# version 19
# baseline (speedup 1.0000x reference)
import numpy as np

import concourse.bass as bass
import concourse.mybir as mybir
from concourse.bass import IndirectOffsetOnAxis
from concourse.bass_utils import run_bass_kernel_spmd

F32 = mybir.dt.float32
U32 = mybir.dt.uint32
AF = mybir.ActivationFunctionType
ALU = mybir.AluOpType
AX = mybir.AxisListType

NEG = -1.0e30
ENGINES = ("sync", "tensor", "vector", "scalar", "gpsimd")


class CFG:
    def __init__(self, n_cores=8, B=16, T=2048, D=1024, H=16, FF=4096, E=8,
                 CAP=2560, eps=1e-5, rope_base=10000.0, debug=False,
                 simtr=False):
        self.n_cores = n_cores
        self.B, self.T, self.D, self.H, self.FF, self.E = B, T, D, H, FF, E
        self.CAP, self.eps, self.rope_base = CAP, eps, rope_base
        self.debug = debug
        self.simtr = simtr
        self.HD = D // H
        self.BL = B // n_cores
        self.NTOK = self.BL * T
        self.NT = self.NTOK // 128
        self.DC = D // 128
        self.FFC = FF // 128
        self.TKC = T // 128
        self.SW = min(512, T)
        self.NV = self.SW // 128
        self.NG = self.TKC // self.NV
        self.HPAIR = H // 2
        self.TTOT = B * T
        self.NSLOT = 2 * CAP
        self.DW = min(512, D)
        self.NDW = D // self.DW
        self.TB = min(512, T)
        self.TBT = self.TB // 128
        self.BS = min(512, self.NSLOT)
        self.BST = self.BS // 128
        self.NSB = self.NSLOT // self.BS
        assert self.HD == 64 and D % 128 == 0 and T % 128 == 0
        assert CAP % 128 == 0 and E == n_cores and B % n_cores == 0
        assert self.TKC % self.NV == 0 and self.NSLOT % self.BS == 0


class Sem:
    __slots__ = ("h", "n", "name")

    def __init__(self, handle, name):
        self.h = handle
        self.n = 0
        self.name = name

    def ev(self):
        return (self, self.n)


class Sched:

    def __init__(self, nc):
        self.nc = nc
        self.q = {e: [] for e in ENGINES}

    def do(self, eng, fn=None, waits=(), inc=None, drain=False):
        for w in waits:
            assert w is None or (isinstance(w, tuple) and
                                 hasattr(w[0], "h")), ("bad wait", w)
        waits = tuple(w for w in waits if w is not None and w[1] > 0)

        def thunk(e, fn=fn, waits=waits, inc=inc, drain=drain):
            if drain:
                e.drain()
            for s, v in waits:
                e.wait_ge(s.h, v)
            r = fn(e) if fn is not None else None
            if inc is not None:
                if r is not None:
                    r.then_inc(inc[0].h, inc[1])
                else:
                    e.sem_inc(inc[0].h, inc[1])

        self.q[eng].append(thunk)

    def ev(self, eng, fn, waits=(), sem=None, dma=False, drain=False):
        amt = 16 if dma else 1
        sem.n += amt
        self.do(eng, fn, waits=waits, inc=(sem, amt), drain=drain)
        return (sem, sem.n)


def _cdiv(a, b):
    return (a + b - 1) // b



def host_prep(cfg, inp, core):
    j = core
    c = cfg
    d = {}
    x = np.ascontiguousarray(inp["x"].reshape(c.B, c.T, c.D))
    d["x_j"] = np.ascontiguousarray(
        x[j * c.BL:(j + 1) * c.BL].reshape(c.NTOK, c.D))
    for nm in ("Wq", "Wk", "Wv"):
        wt = np.ascontiguousarray(inp[nm].T)
        d[nm + "H"] = np.ascontiguousarray(
            wt.reshape(c.DC, 128, c.HPAIR, 128).transpose(2, 1, 0, 3)
            .reshape(c.HPAIR, 128, c.D))
    d["WoT"] = np.ascontiguousarray(inp["Wo"].T)
    for nm in ("bq", "bk"):
        d[nm + "P"] = np.ascontiguousarray(inp[nm].reshape(c.DC, 128).T)
    d["bvB"] = np.ascontiguousarray(
        np.broadcast_to(inp["bv"].reshape(1, c.D), (128, c.D)))
    d["boB"] = np.ascontiguousarray(
        np.broadcast_to(inp["bo"].reshape(1, c.D), (128, c.D)))
    d["gate_wT"] = np.ascontiguousarray(
        inp["gate_w"].T.reshape(c.DC, 128, c.E).transpose(1, 0, 2)
        .reshape(128, c.DC * c.E))
    d["gate_bR"] = np.ascontiguousarray(inp["gate_b"].reshape(1, c.E))
    d["W1H"] = np.ascontiguousarray(
        inp["W1"][j].reshape(c.DC, 128, c.FFC, 128).transpose(2, 1, 0, 3)
        .reshape(c.FFC, 128, c.D))
    d["b1P"] = np.ascontiguousarray(inp["b1"][j].reshape(c.FFC, 128).T)
    d["W2_j"] = np.ascontiguousarray(inp["W2"][j])
    d["b2B"] = np.ascontiguousarray(
        np.broadcast_to(inp["b2"][j].reshape(1, c.D), (128, c.D)))

    hd = c.HD
    half = hd // 2
    theta = (1.0 / c.rope_base **
             (np.arange(0, hd, 2, dtype=np.float32) / np.float32(hd)))
    pos = np.arange(1, c.T + 1, dtype=np.float32)
    ang = pos[:, None] * theta[None, :]
    ang = np.concatenate([ang, ang], axis=1)
    cos = np.cos(ang).T.astype(np.float32)
    sin = np.sin(ang).T.astype(np.float32)
    sgn = np.ones((hd, 1), np.float32)
    sgn[:half] = -1.0
    sinS = (sin * sgn).astype(np.float32)
    d["ropeCos2"] = np.ascontiguousarray(np.concatenate([cos, cos], axis=0))
    d["ropeSin2"] = np.ascontiguousarray(np.concatenate([sinS, sinS], axis=0))
    R = np.zeros((128, 128), np.float32)
    for blk in range(2):
        for fo in range(hd):
            fi = fo + half if fo < half else fo - half
            R[blk * hd + fi, blk * hd + fo] = 1.0
    d["ropeR"] = R

    f = np.arange(c.SW)[None, :]
    pp = np.arange(128)[:, None]
    d["maskdiag"] = np.ascontiguousarray(np.stack(
        [np.where(f <= pp + 128 * v, 0.0, NEG).astype(np.float32)
         for v in range(c.NV)]))
    d["ident128"] = np.eye(128, dtype=np.float32)
    d["ones128"] = np.ones((1, 128), np.float32)
    d["ones_col"] = np.ones((128, 1), np.float32)
    d["ltri"] = np.ascontiguousarray(
        np.tril(np.ones((128, 128), np.float32), -1).T)
    d["iota8"] = np.ascontiguousarray(
        np.broadcast_to(np.arange(c.E, dtype=np.float32), (128, c.E)).copy())
    d["prefmask"] = np.ascontiguousarray(
        (np.arange(c.n_cores) < j).astype(np.float32).reshape(c.n_cores, 1))
    w = c.CAP // 128
    base = j * w + np.minimum(np.arange(128), w - 1)
    nrows = c.E * c.CAP // 128
    d["idxbase"] = np.stack([base, base + nrows], axis=1).astype(np.uint32)
    d["tokid0"] = (j * c.NTOK + np.arange(128, dtype=np.float32)
                   ).reshape(128, 1)
    d["dropidx"] = (2 * c.E * c.CAP + np.arange(128, dtype=np.float32)
                    ).reshape(128, 1)
    return d



def build_program(cfg, stop_after=None):
    c = cfg
    nc = bass.Bass("TRN2", target_bir_lowering=False)

    def din(name, shape, dtype=F32):
        return nc.dram_tensor(name, list(shape), dtype,
                              kind="ExternalInput").ap()

    def dout(name, shape, dtype=F32):
        return nc.dram_tensor(name, list(shape), dtype,
                              kind="ExternalOutput").ap()

    def dtmp(name, shape, dtype=F32):
        return nc.dram_tensor(name, list(shape), dtype, kind="Internal").ap()

    x_j = din("x_j", [c.NTOK, c.D])
    WqH = din("WqH", [c.HPAIR, 128, c.D])
    WkH = din("WkH", [c.HPAIR, 128, c.D])
    WvH = din("WvH", [c.HPAIR, 128, c.D])
    WoT = din("WoT", [c.D, c.D])
    bqP = din("bqP", [128, c.DC])
    bkP = din("bkP", [128, c.DC])
    bvB = din("bvB", [128, c.D])
    boB = din("boB", [128, c.D])
    gate_wT = din("gate_wT", [128, c.DC * c.E])
    gate_bR = din("gate_bR", [1, c.E])
    W1H = din("W1H", [c.FFC, 128, c.D])
    b1P = din("b1P", [128, c.FFC])
    W2_j = din("W2_j", [c.FF, c.D])
    b2B = din("b2B", [128, c.D])
    ropeCos2 = din("ropeCos2", [128, c.T])
    ropeSin2 = din("ropeSin2", [128, c.T])
    ropeR = din("ropeR", [128, 128])
    maskdiag = din("maskdiag", [c.NV, 128, c.SW])
    ident128 = din("ident128", [128, 128])
    ones128 = din("ones128", [1, 128])
    ones_col = din("ones_col", [128, 1])
    ltri = din("ltri", [128, 128])
    iota8 = din("iota8", [128, c.E])
    prefmask = din("prefmask", [c.n_cores, 1])
    idxbase = din("idxbase", [128, 2], U32)
    tokid0 = din("tokid0", [128, 1])
    dropidx = din("dropidx", [128, 1])

    y_j = dout("y_j", [c.NTOK, c.D])

    xn_dram = dtmp("xn_dram", [c.NTOK, c.D])
    xnT_dram = dtmp("xnT_dram", [c.D, c.NTOK])
    attres_dram = dtmp("attres_dram", [c.NTOK, c.D])
    attnT_dram = dtmp("attnT_dram", [c.BL, c.D, c.T])
    cnt_in = dtmp("cnt_in", [1, 2 * c.E])
    cnt_out = dtmp("cnt_out", [c.n_cores, 2 * c.E])
    idtab_loc = dtmp("idtab_loc", [2 * c.E * c.CAP + 128])
    idtab_full = dtmp("idtab_full", [2 * c.E * c.CAP + 128])
    xn_full = dtmp("xn_full", [c.TTOT, c.D])
    yb_loc = dtmp("yb_loc", [c.NSLOT, c.D])
    yb_full = dtmp("yb_full", [c.n_cores * c.NSLOT, c.D])

    dbg = {}
    if c.debug:
        dbg["xn"] = dout("dbg_xn", [c.NTOK, c.D])
        dbg["xnT"] = dout("dbg_xnT", [c.D, c.NTOK])
        dbg["route"] = dout("dbg_route", [c.NT, 128, 10])
        dbg["cnt"] = dout("dbg_cnt", [c.n_cores, 2 * c.E])
        dbg["idtab"] = dout("dbg_idtab", [2, c.E * c.CAP])
        dbg["attres"] = dout("dbg_attres", [c.NTOK, c.D])
        dbg["ybloc"] = dout("dbg_ybloc", [c.NSLOT, c.D])

    cms = []

    def enter(cm):
        v = cm.__enter__()
        cms.append(cm)
        return v

    def sb(name, shape, dtype=F32):
        return enter(nc.sbuf_tensor(name, list(shape), dtype))

    def mksem(name):
        return Sem(enter(nc.semaphore(name)), name)

    try:
        sT, sV, sS, sG = (mksem(n) for n in ("sT", "sV", "sS", "sG"))
        sCst = mksem("sCst")
        sIZ = mksem("sIZ")
        sC = mksem("sC")
        sDBG = mksem("sDBG")
        sXL = [mksem("sXL0"), mksem("sXL1")]
        sXN = [mksem("sXN0"), mksem("sXN1")]
        sXNT = [mksem("sXNT%d" % i) for i in range(c.BL)]
        sAGC = mksem("sAGC")
        sAGX = mksem("sAGX")
        sARI = mksem("sARI")
        sSCT = mksem("sSCT")
        sWS = [mksem("sWS%d" % i) for i in range(4)]
        sXB = [mksem("sXB%d" % i) for i in range(4)]
        sWO = [mksem("sWO%d" % i) for i in range(2)]
        sAF = [mksem("sAF%d" % i) for i in range(2)]
        sXT = mksem("sXT")
        sAS = [mksem("sAS%d" % i) for i in range(2)]
        sOS = [mksem("sOS%d" % i) for i in range(2)]
        sW1 = [mksem("sW1%d" % i) for i in range(4)]
        sW2 = [mksem("sW2%d" % i) for i in range(4)]
        sID = mksem("sID")
        sGA = [mksem("sGA%d" % i) for i in range(6)]
        sYS = [mksem("sYS%d" % i) for i in range(c.NSB)]
        sYAG = mksem("sYAG")
        sAR2 = [mksem("sAR2%d" % i) for i in range(4)]
        sGC = [mksem("sGC%d" % i) for i in range(4)]
        sY = [mksem("sY%d" % i) for i in range(4)]
        all_sems = ([sT, sV, sS, sG, sCst, sIZ, sC, sDBG, sAGC, sAGX, sARI,
                     sSCT, sXT, sID, sYAG] + sXB + sXL + sXN + sXNT + sWS +
                    sWO + sAF + sAS + sOS + sW1 + sW2 + sGA + sYS + sAR2 +
                    sGC + sY)

        p = Sched(nc)

        def mm(out, lhsT, rhs, start, stop):
            return lambda e: e.matmul(out, lhsT, rhs, start=start, stop=stop,
                                      skip_group_check=True)

        def tpose(out, in_, ident, start, stop):
            return lambda e: e.matmul(out, in_, ident, start=start, stop=stop,
                                      is_transpose=True, skip_group_check=True)

        cs_mask = sb("cs_mask", [128, c.NV * c.SW])
        cs_ident = sb("cs_ident", [128, 128])
        cs_ones = sb("cs_ones", [1, 128])
        cs_onesc = sb("cs_onesc", [128, 1])
        cs_ltri = sb("cs_ltri", [128, 128])
        cs_iota8 = sb("cs_iota8", [128, c.E])
        cs_pref = sb("cs_pref", [c.n_cores, 1])
        cs_idxb = sb("cs_idxb", [128, 2], U32)
        cs_tok0 = sb("cs_tok0", [128, 1])
        cs_drop = sb("cs_drop", [128, 1])
        cs_cos = sb("cs_cos", [128, c.T])
        cs_sin = sb("cs_sin", [128, c.T])
        cs_R = sb("cs_R", [128, 128])
        cs_bq = sb("cs_bq", [128, c.DC])
        cs_bk = sb("cs_bk", [128, c.DC])
        cs_bvB = sb("cs_bvB", [128, c.D])
        cs_boB = sb("cs_boB", [128, c.D])
        cs_gw = sb("cs_gw", [128, c.DC * c.E])
        cs_gb = sb("cs_gb", [1, c.E])
        cs_b1 = sb("cs_b1", [128, c.FFC])

        st_p = sb("st_p", [128, 2 * c.NT])
        st_e = sb("st_e", [128, 2 * c.NT])
        st_pos = sb("st_pos", [128, 2 * c.NT])
        st_w = sb("st_w", [128, 2 * c.NT])
        st_row = sb("st_row", [128, 2 * c.NT], U32)
        st_oh = sb("st_oh", [128, c.NT * 2 * c.E])
        cumprev = sb("cumprev", [1, 2 * c.E])
        gstage = sb("gstage", [128, 56])
        offsb = sb("offsb", [128, 2 * c.E])
        statb = sb("statb", [128, 8])
        allcnt = sb("allcnt", [c.n_cores, 2 * c.E])

        pb = enter(nc.psum_tensor("pb", [128, 4096], F32))

        def bank(i, w=None):
            w = 512 if w is None else w
            return pb[:, i * 512:i * 512 + w]

        bkcons = {}

        def bk_wait(i):
            return bkcons.get(i)

        def bk_set(i, ev):
            bkcons[i] = ev

        class O:
            pass

        AT = O()
        off = [0]

        def asl(n):
            o = off[0]
            off[0] += n
            return o

        AT.xnT_b = asl(c.DC * c.T)
        AT.wslot = asl(4 * c.D)
        AT.QT2 = asl(c.T)
        AT.KT2 = asl(c.T)
        AT.V2 = asl(c.T)
        AT.Pbuf = asl(c.NV * c.T)
        AT.PTb = asl(4 * c.SW)
        AT.attst = asl(2 * c.SW)
        AT.lsum = asl(4 * (c.NG + 1) + c.NV)
        AT.x2 = asl(2 * c.D)
        AT.xn2 = asl(2 * c.D)
        AT.xnT1 = asl(c.D)
        AW = off[0]
        if 2 * c.TB + 2 * c.D <= c.NV * c.T and c.TBT <= 4:
            AT.att_fc = AT.Pbuf
            AT.wo2 = AT.Pbuf + 2 * c.TB
            AT.xtile = AT.x2
            AT.ost = AT.xnT1
        else:
            AT.att_fc = asl(2 * c.TB)
            AT.wo2 = asl(2 * c.D)
            AT.xtile = asl(c.TBT * c.D)
            AT.ost = asl(2 * c.DW)
            AW = off[0]
        AT.zb = AT.Pbuf
        off[0] = 0
        MO = O()
        CAPT = c.CAP // 128
        MO.idraw = asl(128)
        MO.idcol = asl(2 * CAPT)
        MO.idu = asl(2 * CAPT)
        MO.xbst = asl(6 * c.D)
        MO.xbT = asl(2 * c.DC * c.BS)
        MO.hmidT = asl(c.FFC * c.BS)
        MO.w1 = asl(4 * c.D)
        MO.w2 = asl(4 * c.D)
        MO.ybst = asl(2 * c.D)
        MO.b2B = asl(c.D)
        MW = off[0]
        MO.g0 = MO.xbst
        MO.g1 = MO.xbst + 4 * c.D
        MO.arb = MO.xbst + 8 * c.D
        MO.outst = MO.xbst + 12 * c.D
        assert 16 * c.D <= 6 * c.D + 2 * c.DC * c.BS + c.FFC * c.BS
        AW = max(AW, MW)
        arena = sb("arena", [128, AW])

        def A(o, n, dtype=None):
            ap = arena[:, o:o + n]
            if dtype is not None:
                ap = ap.bitcast(dtype)
            return ap

        for dst, src in (
            (cs_mask[:].rearrange("p (v w) -> p v w", v=c.NV),
             maskdiag.transpose([1, 0, 2])),
            (cs_ident[:], ident128[:, :]), (cs_ones[:], ones128[:, :]),
            (cs_onesc[:], ones_col[:, :]), (cs_ltri[:], ltri[:, :]),
            (cs_iota8[:], iota8[:, :]), (cs_pref[:], prefmask[:, :]),
            (cs_idxb[:], idxbase[:, :]), (cs_tok0[:], tokid0[:, :]),
            (cs_drop[:], dropidx[:, :]),
            (cs_cos[:], ropeCos2[:, :]), (cs_sin[:], ropeSin2[:, :]),
            (cs_R[:], ropeR[:, :]),
            (cs_bq[:], bqP[:, :]), (cs_bk[:], bkP[:, :]),
            (cs_bvB[:], bvB[:, :]), (cs_boB[:], boB[:, :]),
            (cs_gw[:], gate_wT[:, :]), (cs_gb[:], gate_bR[:, :]),
            (cs_b1[:], b1P[:, :]),
        ):
            p.ev("sync", lambda e, d=dst, s=src: e.dma_start(out=d, in_=s),
                 sem=sCst, dma=True)
        zw = (2 * c.E * c.CAP + 128) // 128
        assert zw <= c.NV * c.T
        zev = p.ev("vector", lambda e: e.memset(A(AT.zb, zw), 0.0), sem=sV)
        idz = p.ev("sync", lambda e: e.dma_start(
            out=idtab_loc.rearrange("(p f) -> p f", p=128),
            in_=A(AT.zb, zw)), waits=[zev], sem=sIZ, dma=True)
        consts_ev = sCst.ev()
        p.do("vector", lambda e: e.memset(cumprev[:], 0.0))

        ldevs = {}
        xn_store_ev = {}
        xnt_tc = {}
        xnt_store_last = {}
        txe_last_ev = {}
        xne_evs = {}
        xnt_ready = {}
        for tt in range(min(2, c.NT)):
            ldevs[tt] = p.ev("sync", lambda e, tt=tt: e.dma_start(
                out=A(AT.x2 + (tt % 2) * c.D, c.D),
                in_=x_j[tt * 128:(tt + 1) * 128, :]), waits=[consts_ev],
                sem=sXL[tt % 2], dma=True)

        sex_prev_box = [None]
        ph1_fin = {}

        def emit_ph1_tile(tt):
            bi = tt % 2
            xt = A(AT.x2 + bi * c.D, c.D)
            xnt = A(AT.xn2 + bi * c.D, c.D)
            p.do("vector", lambda e, xt=xt: e.reduce_sum(
                statb[:, 0:1], xt, axis=AX.X), waits=[ldevs[tt]], drain=True)
            p.do("vector", lambda e: e.tensor_scalar_mul(
                statb[:, 0:1], statb[:, 0:1], 1.0 / c.D), drain=True)
            xg = []
            if tt >= 2:
                xg = [xn_store_ev[tt - 2], txe_last_ev[tt - 2]]
            sqev = p.ev("scalar", lambda e, xt=xt, xnt=xnt: e.activation(
                out=xnt, in_=xt, func=AF.Square,
                accum_out=statb[:, 1:2]),
                waits=[ldevs[tt]] + xg, sem=sS, drain=True)
            p.do("vector", lambda e: e.tensor_scalar(
                statb[:, 2:3], statb[:, 1:2], 1.0 / c.D, float(c.eps),
                ALU.mult, ALU.add), waits=[sqev], drain=True)
            p.do("vector", lambda e: e.tensor_tensor(
                statb[:, 3:4], statb[:, 0:1], statb[:, 0:1], ALU.mult),
                drain=True)
            vev = p.ev("vector", lambda e: e.tensor_tensor(
                statb[:, 2:3], statb[:, 2:3], statb[:, 3:4], ALU.subtract),
                sem=sV, drain=True)
            sqr = p.ev("scalar", lambda e: e.activation(
                out=statb[:, 4:5], in_=statb[:, 2:3], func=AF.Sqrt),
                waits=[vev], sem=sS, drain=True)
            p.do("vector", lambda e: e.reciprocal(
                statb[:, 4:5], statb[:, 4:5]), waits=[sqr], drain=True)
            p.do("vector", lambda e: e.tensor_tensor(
                statb[:, 5:6], statb[:, 0:1], statb[:, 4:5], ALU.mult),
                drain=True)
            vst = p.ev("vector", lambda e: e.tensor_scalar_mul(
                statb[:, 5:6], statb[:, 5:6], -1.0), sem=sV, drain=True)
            xne = p.ev("scalar", lambda e, xt=xt, xnt=xnt: e.activation(
                out=xnt, in_=xt, func=AF.Identity, bias=statb[:, 5:6],
                scale=statb[:, 4:5]), waits=[vst], sem=sS, drain=True)
            xne_evs[tt] = xne
            if tt + 2 < c.NT:
                ldevs[tt + 2] = p.ev("sync", lambda e, tt=tt: e.dma_start(
                    out=A(AT.x2 + (tt % 2) * c.D, c.D),
                    in_=x_j[(tt + 2) * 128:(tt + 3) * 128, :]), waits=[xne],
                    sem=sXL[tt % 2], dma=True)
            xn_store_ev[tt] = p.ev("sync", lambda e, xnt=xnt, tt=tt:
                                   e.dma_start(
                                       out=xn_dram[tt * 128:(tt + 1) * 128,
                                                   :], in_=xnt),
                                   waits=[xne], sem=sXN[tt % 2], dma=True)
            if c.debug:
                xn_store_ev[tt] = p.ev("sync", lambda e, xnt=xnt, tt=tt:
                                       e.dma_start(
                                           out=dbg["xn"][tt * 128:
                                                         (tt + 1) * 128, :],
                                           in_=xnt),
                                       waits=[xne], sem=sXN[tt % 2],
                                       dma=True)
            tevs = []
            for g in range(_cdiv(c.DC, 4)):
                n = min(4, c.DC - g * 4)
                tw = [xne] if g == 0 else []
                if g == 0 and tt > 0:
                    tw.append(xnt_store_last[tt - 1])
                if g == 0:
                    tw.append(bk_wait(7))
                if tevs:
                    tw.append(tevs[-1])
                te = None
                for i in range(n):
                    dc = g * 4 + i
                    if i < n - 1:
                        p.do("tensor", tpose(
                            bank(7)[:, i * 128:(i + 1) * 128],
                            xnt[:, dc * 128:(dc + 1) * 128], cs_ident[:],
                            i == 0, False), waits=tw if i == 0 else ())
                    else:
                        te = p.ev("tensor", tpose(
                            bank(7)[:, i * 128:(i + 1) * 128],
                            xnt[:, dc * 128:(dc + 1) * 128], cs_ident[:],
                            i == 0, True), waits=tw if i == 0 else (),
                            sem=sT)
                tevs.append(p.ev("vector", lambda e, g=g, n=n: e.tensor_copy(
                    A(AT.xnT1 + g * 512, n * 128), bank(7)[:, :n * 128]),
                    waits=[te], sem=sV, drain=True))
            bk_set(7, tevs[-1])
            txe_last_ev[tt] = tevs[-1]
            for dc in range(c.DC):
                ev = p.ev("sync", lambda e, dc=dc, tt=tt: e.dma_start(
                    out=xnT_dram[dc * 128:(dc + 1) * 128,
                                 tt * 128:(tt + 1) * 128],
                    in_=A(AT.xnT1 + dc * 128, 128)), waits=[tevs[-1]],
                    sem=sXNT[tt // c.TKC], dma=True)
                if c.debug:
                    xnt_store_last[tt] = p.ev(
                        "sync", lambda e, dc=dc, tt=tt: e.dma_start(
                            out=dbg["xnT"][dc * 128:(dc + 1) * 128,
                                           tt * 128:(tt + 1) * 128],
                            in_=A(AT.xnT1 + dc * 128, 128)),
                        waits=[tevs[-1]], sem=sDBG, dma=True)
                else:
                    xnt_store_last[tt] = ev
            tpc = c.SW // 128
            if (tt + 1) % tpc == 0:
                bb = tt // c.TKC
                tcc = ((tt % c.TKC) + 1) // tpc - 1
                xnt_tc[(bb, tcc)] = sXNT[bb].ev()

            for dc in range(c.DC):
                gw0 = [tevs[dc // 4]] if dc % 4 == 0 else []
                if dc == 0:
                    gw0.append(bk_wait(4))
                p.do("tensor", mm(
                    bank(4, c.E), A(AT.xnT1 + dc * 128, 128),
                    cs_gw[:, dc * c.E:(dc + 1) * c.E], dc == 0, False),
                    waits=gw0)
            gmm = p.ev("tensor", mm(bank(4, c.E), cs_ones[0:1, :],
                                    cs_gb[0:1, :], False, True), sem=sT)
            sex_prev = sex_prev_box[0]
            p.do("vector", lambda e: e.reduce_max(
                gstage[:, 0:1], bank(4, c.E), axis=AX.X),
                waits=[gmm] + ([sex_prev] if sex_prev else []))
            vex = p.ev("vector", lambda e: e.tensor_scalar_mul(
                gstage[:, 0:1], gstage[:, 0:1], -1.0), sem=sV, drain=True)
            sex = p.ev("scalar", lambda e: e.activation(
                out=gstage[:, 8:8 + c.E], in_=bank(4, c.E), func=AF.Exp,
                bias=gstage[:, 0:1], accum_out=gstage[:, 1:2]),
                waits=[vex], sem=sS)
            sex_prev_box[0] = sex
            bk_set(4, sex)
            p.do("vector", lambda e: e.reciprocal(
                gstage[:, 1:2], gstage[:, 1:2]), waits=[sex], drain=True)
            p.do("vector", lambda e: e.tensor_scalar(
                gstage[:, 8:8 + c.E], gstage[:, 8:8 + c.E], gstage[:, 1:2],
                None, ALU.mult), drain=True)
            p.do("vector", lambda e: e.max(
                gstage[:, 16:24], gstage[:, 8:8 + c.E]), drain=True)
            p.do("vector", lambda e: e.max_index(
                gstage[:, 24:32].bitcast(U32), gstage[:, 16:24],
                gstage[:, 8:8 + c.E]), drain=True)
            p.do("vector", lambda e: e.tensor_copy(
                gstage[:, 26:28], gstage[:, 24:26].bitcast(U32)), drain=True)
            for ch in range(2):
                col = 2 * tt + ch
                p.do("vector", lambda e, ch=ch, col=col: e.tensor_copy(
                    st_p[:, col:col + 1], gstage[:, 16 + ch:17 + ch]),
                    drain=(ch == 0))
                p.do("vector", lambda e, ch=ch, col=col: e.tensor_copy(
                    st_e[:, col:col + 1], gstage[:, 26 + ch:27 + ch]))
                p.do("vector", lambda e, ch=ch, col=col: e.tensor_scalar(
                    st_oh[:, col * c.E:(col + 1) * c.E],
                    cs_iota8[:], gstage[:, 26 + ch:27 + ch], None,
                    ALU.is_equal))
            ohev = p.ev("vector", lambda e: e.drain(), sem=sV)
            ohsl = st_oh[:, 2 * tt * c.E:(2 * tt + 2) * c.E]
            p.do("tensor", mm(bank(5, 2 * c.E), cs_ltri[:], ohsl,
                              True, False), waits=[ohev, bk_wait(5)])
            pmm = p.ev("tensor", mm(bank(5, 2 * c.E), cs_ones[0:1, :],
                                    cumprev[:], False, True), sem=sT)
            cpe = p.ev("tensor", mm(bank(6)[0:1, 0:2 * c.E], cs_onesc[:],
                                    ohsl, True, True),
                       waits=[bk_wait(6)], sem=sT)
            cpv = p.ev("vector", lambda e: e.tensor_tensor(
                cumprev[:], cumprev[:], bank(6)[0:1, 0:2 * c.E], ALU.add),
                waits=[cpe], sem=sV, drain=True)
            bk_set(6, cpv)
            pos_last = None
            for ch in range(2):
                col = 2 * tt + ch
                p.do("vector", lambda e, col=col, ch=ch: e.tensor_tensor(
                    gstage[:, 32:32 + c.E],
                    st_oh[:, col * c.E:(col + 1) * c.E],
                    bank(5, 2 * c.E)[:, ch * c.E:(ch + 1) * c.E], ALU.mult),
                    waits=[pmm], drain=True)
                pos_last = p.ev("vector", lambda e, col=col: e.reduce_sum(
                    st_pos[:, col:col + 1], gstage[:, 32:32 + c.E],
                    axis=AX.X), sem=sV, drain=True)
            bk_set(5, pos_last)
            if tt == c.NT - 1:
                emit_ph1_fin()

        def emit_ph1_fin():
            ph1_fin["v_end"] = p.ev("vector", lambda e: e.drain(), sem=sV)
            ph1_fin["s_end"] = (sS, sS.n)
            ph1_fin["sdone"] = [s.ev() for s in sXN]
            ph1_fin["tdone"] = [s.ev() for s in sXNT]
            cpy = p.ev("sync", lambda e: e.dma_start(out=cnt_in[:, :],
                                                     in_=cumprev[:]),
                       waits=[ph1_fin["v_end"]], sem=sC, dma=True)
            ph1_fin["ag_cnt"] = p.ev("gpsimd", lambda e: e.collective_compute(
                "AllGather", ALU.bypass,
                replica_groups=[list(range(c.n_cores))],
                ins=[cnt_in[:, :]], outs=[cnt_out[:, :]]),
                waits=[cpy], sem=sAGC)
            ph1_fin["ag_xn"] = p.ev("gpsimd", lambda e: e.collective_compute(
                "AllGather", ALU.bypass,
                replica_groups=[list(range(c.n_cores))],
                ins=[xn_dram[:, :]], outs=[xn_full[:, :]]),
                waits=ph1_fin["sdone"], sem=sAGX)
            ph1_fin["acl"] = p.ev("sync", lambda e: e.dma_start(
                out=allcnt[:], in_=cnt_out[:, :]),
                waits=[ph1_fin["ag_cnt"]], sem=sC, dma=True)
            if c.debug:
                p.ev("sync", lambda e: e.dma_start(
                    out=dbg["cnt"][:, :], in_=allcnt[:]),
                    waits=[ph1_fin["acl"]], sem=sDBG, dma=True)

        ph1_head = c.TKC if c.BL > 1 else c.NT
        for tt in range(ph1_head):
            emit_ph1_tile(tt)
        ph1_next = [ph1_head]

        def emit_ph1_chunk(k):
            tgt = min(c.NT, ph1_next[0] + k)
            while ph1_next[0] < tgt:
                emit_ph1_tile(ph1_next[0])
                ph1_next[0] += 1

        if stop_after == "ph1":
            while ph1_next[0] < c.NT:
                emit_ph1_tile(ph1_next[0])
                ph1_next[0] += 1
            _finish(p, all_sems)
            _emit(nc, p)
            return nc

        def emit_routing():
            tof = p.ev("tensor", mm(bank(4)[0:1, 0:2 * c.E], cs_pref[:],
                                    allcnt[:], True, True),
                       waits=[ph1_fin["acl"], bk_wait(4), (sV, sV.n)],
                       sem=sT)
            cpo = p.ev("vector", lambda e: e.tensor_copy(
                gstage[0:1, 32:32 + 2 * c.E], bank(4)[0:1, 0:2 * c.E]),
                waits=[tof], sem=sV)
            bro = p.ev("tensor", mm(bank(4, 2 * c.E), cs_ones[0:1, :],
                                    gstage[0:1, 32:32 + 2 * c.E],
                                    True, True), waits=[cpo], sem=sT)
            offs_ev = p.ev("vector", lambda e: e.tensor_copy(
                offsb[:], bank(4, 2 * c.E)), waits=[bro], sem=sV)
            bk_set(4, offs_ev)
            BSC = 512
            g = gstage
            sct_evs = []
            for tt in range(c.NT):
                for ch in range(2):
                    col = 2 * tt + ch
                    ohsl = st_oh[:, col * c.E:(col + 1) * c.E]
                    p.do("gpsimd", lambda e, ohsl=ohsl, ch=ch: e.tensor_tensor(
                        g[:, 32:32 + c.E], ohsl,
                        offsb[:, ch * c.E:(ch + 1) * c.E], ALU.mult),
                        waits=[offs_ev]
                        if tt == 0 and ch == 0 else (), drain=True)
                    p.do("gpsimd", lambda e: e.tensor_tensor(
                        g[:, 32:36], g[:, 32:36], g[:, 36:40], ALU.add),
                        drain=True)
                    p.do("gpsimd", lambda e: e.tensor_tensor(
                        g[:, 32:34], g[:, 32:34], g[:, 34:36], ALU.add),
                        drain=True)
                    p.do("gpsimd", lambda e: e.tensor_tensor(
                        g[:, 40:41], g[:, 32:33], g[:, 33:34], ALU.add),
                        drain=True)
                    p.do("gpsimd", lambda e, col=col: e.tensor_tensor(
                        st_pos[:, col:col + 1], st_pos[:, col:col + 1],
                        g[:, 40:41], ALU.add), drain=True)
                    p.do("gpsimd", lambda e, col=col: e.tensor_scalar(
                        g[:, 41:42], st_pos[:, col:col + 1],
                        float(c.CAP), None, ALU.is_lt), drain=True)
                    p.do("gpsimd", lambda e, col=col: e.tensor_tensor(
                        st_w[:, col:col + 1], st_p[:, col:col + 1],
                        g[:, 41:42], ALU.mult), drain=True)
                    p.do("gpsimd", lambda e, col=col, ch=ch: e.tensor_scalar(
                        g[:, 42:43], st_e[:, col:col + 1], float(c.CAP),
                        float(ch * c.E * c.CAP), ALU.mult, ALU.add))
                    p.do("gpsimd", lambda e, col=col: e.tensor_tensor(
                        g[:, 42:43], g[:, 42:43], st_pos[:, col:col + 1],
                        ALU.add), drain=True)
                    p.do("gpsimd", lambda e: e.tensor_scalar(
                        g[:, 42:43], g[:, 42:43], g[:, 41:42],
                        None, ALU.mult), drain=True)
                    p.do("gpsimd", lambda e: e.tensor_scalar(
                        g[:, 43:44], g[:, 41:42], -1.0, 1.0,
                        ALU.mult, ALU.add), drain=True)
                    p.do("gpsimd", lambda e: e.tensor_tensor(
                        g[:, 43:44], g[:, 43:44], cs_drop[:], ALU.mult),
                        drain=True)
                    p.do("gpsimd", lambda e, col=col: e.tensor_tensor(
                        g[:, 48:49], g[:, 42:43],
                        g[:, 43:44], ALU.add), drain=True)
                    p.do("gpsimd", lambda e, col=col, ch=ch: e.tensor_scalar(
                        g[:, 44:45], st_pos[:, col:col + 1], 1.0,
                        float(ch * c.CAP), ALU.mult, ALU.add))
                    p.do("gpsimd", lambda e: e.tensor_scalar_min(
                        g[:, 44:45], g[:, 44:45], float(c.NSLOT - 1)),
                        drain=True)
                    p.do("gpsimd", lambda e: e.tensor_copy(
                        g[:, 45:46].bitcast(U32), g[:, 44:45]), drain=True)
                    p.do("gpsimd", lambda e: e.tensor_scalar(
                        g[:, 52:53], g[:, 44:45], 1.0 / BSC,
                        -255.5 / BSC, ALU.mult, ALU.add), drain=True)
                    p.do("gpsimd", lambda e: e.tensor_copy(
                        g[:, 46:47].bitcast(U32), g[:, 52:53]), drain=True)
                    p.do("gpsimd", lambda e: e.tensor_scalar(
                        g[:, 53:54].bitcast(U32), g[:, 46:47].bitcast(U32),
                        BSC, None, ALU.mult), drain=True)
                    p.do("gpsimd", lambda e: e.tensor_tensor(
                        g[:, 47:48].bitcast(U32), g[:, 45:46].bitcast(U32),
                        g[:, 53:54].bitcast(U32), ALU.subtract), drain=True)
                    p.do("gpsimd", lambda e: e.tensor_scalar(
                        g[:, 46:47].bitcast(U32), g[:, 46:47].bitcast(U32),
                        c.n_cores * BSC, None, ALU.mult), drain=True)
                    p.do("gpsimd", lambda e, col=col: e.tensor_copy(
                        g[:, 49:50].bitcast(U32), st_e[:, col:col + 1]),
                        drain=True)
                    p.do("gpsimd", lambda e: e.tensor_scalar(
                        g[:, 49:50].bitcast(U32), g[:, 49:50].bitcast(U32),
                        BSC, None, ALU.mult), drain=True)
                    p.do("gpsimd", lambda e: e.tensor_tensor(
                        g[:, 46:47].bitcast(U32), g[:, 46:47].bitcast(U32),
                        g[:, 49:50].bitcast(U32), ALU.add), drain=True)
                    p.do("gpsimd", lambda e, col=col: e.tensor_tensor(
                        st_row[:, col:col + 1], g[:, 46:47].bitcast(U32),
                        g[:, 47:48].bitcast(U32), ALU.add), drain=True)
                    sc = 50 if ch == 0 else 54
                    p.do("gpsimd", lambda e, tt=tt, sc=sc:
                         e.tensor_scalar_add(
                             g[:, sc:sc + 1], cs_tok0[:], float(128 * tt)),
                         waits=[sct_evs[-2]] if len(sct_evs) >= 2 else (),
                         drain=True)
                    p.do("gpsimd", lambda e, col=col, sc=sc: e.tensor_copy(
                        g[:, sc + 1:sc + 2].bitcast(U32), g[:, 48:49]),
                        drain=True)
                    sct_evs.append(p.ev(
                        "gpsimd", lambda e, sc=sc: e.indirect_dma_start(
                            out=idtab_loc.rearrange("(a b) -> a b", b=1),
                            out_offset=IndirectOffsetOnAxis(
                                g[:, sc + 1:sc + 2].bitcast(U32), 0),
                            in_=g[:, sc:sc + 1], in_offset=None),
                        waits=([idz] if tt == 0 and ch == 0
                               else [sct_evs[-1]]),
                        sem=sSCT, dma=True, drain=True))
            scat_done = sSCT.ev()
            rt_done = p.ev("gpsimd", lambda e: e.drain(), sem=sG)
            rt_done_box[0] = rt_done
            ar_id = p.ev("gpsimd", lambda e: e.collective_compute(
                "AllReduce", ALU.add,
                replica_groups=[list(range(c.n_cores))],
                ins=[idtab_loc[:]], outs=[idtab_full[:]]),
                waits=[scat_done], sem=sARI)
            if c.debug:
                p.ev("sync", lambda e: e.dma_start(
                    out=dbg["idtab"][:, :],
                    in_=idtab_full[0:2 * c.E * c.CAP].rearrange(
                        "(a b) -> a b", a=2)), waits=[ar_id], sem=sDBG,
                    dma=True)
            return ar_id

        rt_done_box = [None]

        if stop_after == "route":
            emit_routing()
            _finish(p, all_sems)
            _emit(nc, p)
            return nc

        xnT_b = A(AT.xnT_b, c.DC * c.T)
        QT2 = A(AT.QT2, c.T)
        KT2 = A(AT.KT2, c.T)
        V2 = A(AT.V2, c.T)
        Pbuf = A(AT.Pbuf, c.NV * c.T)
        lsum = A(AT.lsum, 4 * (c.NG + 1) + c.NV)
        SW = c.SW
        LS_R = 4 * c.NG

        wsrc = []
        for b in range(c.BL):
            for hp in range(c.HPAIR):
                wsrc += [(WqH, hp), (WkH, hp), (WvH, hp)]
        wload_evs = {}
        wcons_evs = {}

        def wload(idx, waits=()):
            W, hp = wsrc[idx]
            slot = idx % 4
            wload_evs[idx] = p.ev(
                "scalar", lambda e, W=W, hp=hp, slot=slot: e.dma_start(
                    out=A(AT.wslot + slot * c.D, c.D), in_=W[hp]),
                waits=list(waits), sem=sWS[idx % 4], dma=True)

        for idx in range(min(4, len(wsrc))):
            wload(idx, waits=[consts_ev] if idx == 0 else ())

        def wbuf(idx):
            return A(AT.wslot + (idx % 4) * c.D, c.D)

        xb_tc = {}
        xb_evs = {}

        def xnT_load(b, extra=()):
            for tc in range(c.T // SW):
                for dc in range(c.DC):
                    xb_tc[(b, tc)] = p.ev(
                        "sync", lambda e, dc=dc, b=b, tc=tc: e.dma_start(
                            out=xnT_b[:, dc * c.T + tc * SW:
                                      dc * c.T + (tc + 1) * SW],
                            in_=xnT_dram[dc * 128:(dc + 1) * 128,
                                         b * c.T + tc * SW:
                                         b * c.T + (tc + 1) * SW]),
                        waits=[xnt_tc[(b, tc)]] + list(extra)
                        if dc == 0 else (),
                        sem=sXB[tc % 4], dma=True)
            return xb_tc[(b, c.T // SW - 1)]

        routing_done = [None]
        wo_free = [None, None]
        af_free = [None, None]
        gctr = [0]
        attst_evac_evs = {}
        attst_store_evs = {}
        o_stores = {}
        af_done = {}
        avt_prev_last = None
        sc_rot = [0]
        last_xnTb_reader = [None]
        for b in range(c.BL):
            xb_evs[b] = xnT_load(
                b, extra=[last_xnTb_reader[0]] if b > 0 else ())
            xb_ev = xb_evs[b]
            for hp in range(c.HPAIR):
                sidx = b * c.HPAIR + hp
                widx = 3 * sidx
                wq, wk, wv = wbuf(widx), wbuf(widx + 1), wbuf(widx + 2)
                qk_evac = []
                rope_v = []
                for pi, (wsl, dstb, biasb) in enumerate(
                        ((wq, QT2, cs_bq), (wk, KT2, cs_bk))):
                    for tc in range(c.T // SW):
                        bk = tc % 2
                        gi = pi * (c.T // SW) + tc
                        w0 = [bk_wait(bk), xb_tc[(b, tc)]]
                        if gi == 0:
                            w0.append(wload_evs[widx + pi])
                        if tc == 0 and pi == 1:
                            w0.append(wload_evs[widx + 1])
                        lastmm = None
                        for dc in range(c.DC):
                            if dc < c.DC - 1:
                                p.do("tensor", mm(
                                    bank(bk, SW),
                                    wsl[:, dc * 128:(dc + 1) * 128],
                                    xnT_b[:, dc * c.T + tc * SW:
                                          dc * c.T + (tc + 1) * SW],
                                    dc == 0, False),
                                    waits=w0 if dc == 0 else ())
                            else:
                                lastmm = p.ev("tensor", mm(
                                    bank(bk, SW),
                                    wsl[:, dc * 128:(dc + 1) * 128],
                                    xnT_b[:, dc * c.T + tc * SW:
                                          dc * c.T + (tc + 1) * SW],
                                    False, True),
                                    waits=w0 if c.DC == 1 else (), sem=sT)
                        qe = p.ev(
                            "scalar",
                            lambda e, dstb=dstb, biasb=biasb, hp=hp, tc=tc,
                            bk=bk: e.activation(
                                out=dstb[:, tc * SW:(tc + 1) * SW],
                                in_=bank(bk, SW), func=AF.Identity,
                                bias=biasb[:, hp:hp + 1]),
                            waits=[lastmm], sem=sS)
                        qk_evac.append(qe)
                        bk_set(bk, qe)
                    for tc in range(c.T // SW):
                        bk = 2 + tc % 2
                        rw = [qk_evac[pi * (c.T // SW) + tc], bk_wait(bk)]
                        rte = p.ev("tensor", mm(
                            bank(bk, SW), cs_R[:],
                            dstb[:, tc * SW:(tc + 1) * SW], True, True),
                            waits=rw, sem=sT)
                        p.do("vector",
                             lambda e, dstb=dstb, tc=tc: e.tensor_tensor(
                                 dstb[:, tc * SW:(tc + 1) * SW],
                                 dstb[:, tc * SW:(tc + 1) * SW],
                                 cs_cos[:, tc * SW:(tc + 1) * SW], ALU.mult),
                             waits=[rte])
                        p.do("vector", lambda e, bk=bk, tc=tc:
                             e.tensor_tensor(
                                 bank(bk, SW), bank(bk, SW),
                                 cs_sin[:, tc * SW:(tc + 1) * SW], ALU.mult))
                        rv = p.ev(
                            "vector",
                            lambda e, dstb=dstb, tc=tc, bk=bk:
                            e.tensor_tensor(
                                dstb[:, tc * SW:(tc + 1) * SW],
                                dstb[:, tc * SW:(tc + 1) * SW],
                                bank(bk, SW), ALU.add), sem=sV, drain=True)
                        rope_v.append(rv)
                        bk_set(bk, rv)
                rope_all = rope_v[-1]
                qk_done = qk_evac[-1]
                v_evac = []
                v_mm = []
                for kt2 in range(c.TKC // 2):
                    bk = kt2 % 2
                    w0 = [bk_wait(bk)]
                    if kt2 >= 2:
                        w0.append(v_mm[kt2 - 1])
                    elif kt2 == 0:
                        w0 += [wload_evs[widx + 2], xb_ev]
                    lastmm = None
                    for sub in range(2):
                        kt = kt2 * 2 + sub
                        for dc in range(c.DC):
                            last = (sub == 1 and dc == c.DC - 1)
                            if not last:
                                p.do("tensor", mm(
                                    bank(bk)[:, sub * 128:(sub + 1) * 128],
                                    xnT_b[:, dc * c.T + kt * 128:
                                          dc * c.T + (kt + 1) * 128],
                                    wv[:, dc * 128:(dc + 1) * 128],
                                    dc == 0, dc == c.DC - 1),
                                    waits=w0 if sub == 0 and dc == 0 else ())
                            else:
                                lastmm = p.ev("tensor", mm(
                                    bank(bk)[:, sub * 128:(sub + 1) * 128],
                                    xnT_b[:, dc * c.T + kt * 128:
                                          dc * c.T + (kt + 1) * 128],
                                    wv[:, dc * 128:(dc + 1) * 128],
                                    dc == 0, True), sem=sT)
                    v_mm.append(lastmm)
                    for sub in range(2):
                        kt = kt2 * 2 + sub
                        ve = p.ev(
                            "vector", lambda e, kt=kt, bk=bk, sub=sub, hp=hp:
                            e.tensor_tensor(
                                V2[:, kt * 128:(kt + 1) * 128],
                                bank(bk)[:, sub * 128:(sub + 1) * 128],
                                cs_bvB[:, hp * 128:(hp + 1) * 128],
                                ALU.add), waits=[lastmm], sem=sV)
                        v_evac.append(ve)
                    bk_set(bk, ve)
                v_done = v_evac[-1]
                last_xnTb_reader[0] = v_mm[-1]
                if widx + 4 < len(wsrc):
                    wload(widx + 4, waits=[qk_evac[c.T // SW - 1]])
                if widx + 5 < len(wsrc):
                    wload(widx + 5, waits=[qk_done])
                if widx + 6 < len(wsrc):
                    wload(widx + 6, waits=[v_mm[-1]])
                first_box = [True]
                sched = []
                for h2x in range(2):
                    for gx in range(c.NG):
                        for qix in range(c.NV):
                            for kcx in range(gx + 1):
                                sched.append((h2x, gx, qix, kcx))
                exp_by_idx = {}
                pend = []
                mm_done = [0]

                def emit_score_mm(idx):
                    h2x, gx, qix, kcx = sched[idx]
                    hbx = h2x * 64
                    nchx = gx + 1
                    qtx = gx * c.NV + qix
                    bkx = sc_rot[0] % 4
                    sc_rot[0] += 1
                    w0 = [bk_wait(bkx)]
                    ew_extra = []
                    if first_box[0]:
                        w0.append(rope_all)
                        if b == 0 and sidx == 0:
                            ew_extra.append(idz)
                        if b > 0 and sidx == b * c.HPAIR:
                            ew_extra += list(af_done[b - 1])
                        first_box[0] = False
                    smm = p.ev("tensor", mm(
                        bank(bkx, SW),
                        QT2[hbx:hbx + 64, qtx * 128:(qtx + 1) * 128],
                        KT2[hbx:hbx + 64, kcx * SW:(kcx + 1) * SW],
                        True, True), waits=w0, sem=sT)
                    if kcx == nchx - 1:
                        gate = p.ev(
                            "vector", lambda e, bkx=bkx, qix=qix:
                            e.tensor_tensor(
                                bank(bkx, SW), bank(bkx, SW),
                                cs_mask[:, qix * SW:(qix + 1) * SW],
                                ALU.add), waits=[smm], sem=sV)
                    else:
                        gate = smm
                    mm_done[0] += 1
                    return (idx, qix, kcx, gate, bkx, ew_extra)

                def emit_score_exp(rec, extra=()):
                    idx, qix, kcx, gate, bkx, ew = rec
                    ev = p.ev(
                        "scalar",
                        lambda e, qix=qix, kcx=kcx, bkx=bkx: e.activation(
                            out=Pbuf[:, qix * c.T + kcx * SW:
                                     qix * c.T + (kcx + 1) * SW],
                            in_=bank(bkx, SW), func=AF.Exp, scale=0.125,
                            accum_out=lsum[:, qix * c.NG + kcx:
                                           qix * c.NG + kcx + 1]),
                        waits=[gate] + ew + list(extra), sem=sS)
                    bk_set(bkx, ev)
                    exp_by_idx[idx] = ev

                ci = 0
                for h2 in range(2):
                    hb = h2 * 64
                    pt_evac = {}
                    avt_mm = {}
                    for g in range(c.NG):
                        nch = g + 1
                        dgevs = []
                        dg2evs = []
                        for qi in range(c.NV):
                            for kc in range(nch):
                                if ci >= mm_done[0]:
                                    rec = emit_score_mm(ci)
                                    emit_score_exp(rec)
                                ci += 1
                            p.do("vector", lambda e, qi=qi, nch=nch:
                                 e.reduce_sum(
                                     lsum[:, LS_R + qi:LS_R + qi + 1],
                                     lsum[:, qi * c.NG:qi * c.NG + nch],
                                     axis=AX.X),
                                 waits=[exp_by_idx[ci - 1]])
                            rle = p.ev("vector", lambda e, qi=qi:
                                       e.reciprocal(
                                           lsum[:, LS_R + qi:LS_R + qi + 1],
                                           lsum[:, LS_R + qi:LS_R + qi + 1]),
                                       sem=sV, drain=True)
                            hd1 = min(256, nch * SW)
                            dgevs.append(p.ev(
                                "scalar",
                                lambda e, qi=qi, hd1=hd1: e.activation(
                                    out=Pbuf[:, qi * c.T:qi * c.T + hd1],
                                    in_=Pbuf[:, qi * c.T:qi * c.T + hd1],
                                    func=AF.Identity,
                                    scale=lsum[:, LS_R + qi:
                                               LS_R + qi + 1]),
                                waits=[rle], sem=sS, drain=True))
                            if nch * SW > hd1:
                                dg2evs.append(p.ev(
                                    "scalar",
                                    lambda e, qi=qi, nch=nch, hd1=hd1:
                                    e.activation(
                                        out=Pbuf[:, qi * c.T + hd1:
                                                 qi * c.T + nch * SW],
                                        in_=Pbuf[:, qi * c.T + hd1:
                                                 qi * c.T + nch * SW],
                                        func=AF.Identity,
                                        scale=lsum[:, LS_R + qi:
                                                   LS_R + qi + 1]),
                                    sem=sS, drain=True))
                            else:
                                dg2evs.append(dgevs[-1])
                        npre = min(4, len(sched) - mm_done[0])
                        for _ in range(npre):
                            pend.append(emit_score_mm(mm_done[0]))
                        gc = gctr[0]
                        nkc = nch * c.NV
                        av_bank = 6 + gc % 2
                        ptm_last = None
                        for step in range(nkc + 1):
                            kc = step
                            if kc < nkc:
                                bk = 4 + kc % 2
                                w0 = []
                                if kc < 2:
                                    w0.append(bk_wait(bk))
                                if kc == 1:
                                    w0.append(dgevs[-1])
                                if kc == 2:
                                    w0.append(dg2evs[-1])
                                if kc >= 2:
                                    w0.append(pt_evac[kc - 2])
                                ptm = None
                                for qi in range(c.NV):
                                    qw = list(w0) if qi == 0 else []
                                    if kc == 0:
                                        qw.append(dgevs[qi])
                                    if qi < c.NV - 1:
                                        p.do("tensor", tpose(
                                            bank(bk)[:, qi * 128:
                                                     (qi + 1) * 128],
                                            Pbuf[:, qi * c.T + kc * 128:
                                                 qi * c.T + (kc + 1) * 128],
                                            cs_ident[:], qi == 0, False),
                                            waits=qw)
                                    else:
                                        ptm = p.ev("tensor", tpose(
                                            bank(bk)[:, qi * 128:
                                                     (qi + 1) * 128],
                                            Pbuf[:, qi * c.T + kc * 128:
                                                 qi * c.T + (kc + 1) * 128],
                                            cs_ident[:], qi == 0, True),
                                            waits=qw, sem=sT)
                                pw = [ptm]
                                if kc >= 3:
                                    pw.append(avt_mm[kc - 3])
                                elif avt_prev_last is not None:
                                    pw.append(avt_prev_last)
                                pt_evac[kc] = p.ev(
                                    "vector", lambda e, kc=kc, bk=bk:
                                    e.tensor_copy(
                                        A(AT.PTb + (kc % 4) * SW, SW),
                                        bank(bk, SW)), waits=pw, sem=sV)
                                bk_set(bk, pt_evac[kc])
                                ptm_last = ptm
                            ac = step - 1
                            if ac >= 0:
                                aw = [pt_evac[ac]]
                                if ac == 0:
                                    aw.append(v_done)
                                    aw.append(bk_wait(av_bank))
                                last = ac == nkc - 1
                                avt_mm[ac] = p.ev("tensor", mm(
                                    bank(av_bank)[0:64, 0:SW],
                                    V2[:, ac * 128 + hb:ac * 128 + hb + 64],
                                    A(AT.PTb + (ac % 4) * SW, SW),
                                    ac == 0, last), waits=aw, sem=sT)
                        amm = avt_mm[nkc - 1]
                        avt_prev_last = amm
                        for j, rec in enumerate(pend):
                            emit_score_exp(rec,
                                           extra=[ptm_last] if j == 0 else ())
                        pend = []

                        avw = [amm, attst_store_evs[gc - 2]
                               if gc >= 2 else None]
                        attst_evac_evs[gc] = p.ev(
                            "vector", lambda e, gc=gc, av_bank=av_bank:
                            e.tensor_copy(
                                A(AT.attst + (gc % 2) * SW, SW)[0:64, :],
                                bank(av_bank)[0:64, 0:SW]),
                            waits=avw, sem=sV)
                        bk_set(av_bank, attst_evac_evs[gc])
                        attst_store_evs[gc] = p.ev(
                            "sync", lambda e, hp=hp, hb=hb, g=g, b=b, gc=gc:
                            e.dma_start(
                                out=attnT_dram[
                                    b, hp * 128 + hb:hp * 128 + hb + 64,
                                    g * SW:(g + 1) * SW],
                                in_=A(AT.attst + (gc % 2) * SW,
                                      SW)[0:64, :]),
                            waits=[attst_evac_evs[gc]],
                            sem=sAS[gc % 2], dma=True)
                        gctr[0] += 1
                rt_at = c.HPAIR if c.BL > 1 else 0
                if sidx == rt_at and stop_after is None:
                    routing_done[0] = emit_routing()
                nhp_i = min(c.HPAIR, 6)
                if b == 0 and c.BL > 1 and hp < nhp_i:
                    emit_ph1_chunk(_cdiv(c.NT - c.TKC, nhp_i))
            att_done = [sAS[0].ev(), sAS[1].ev()]
            nfc = c.DC
            NGR = (c.T // c.TB) * nfc
            wo_evs = {}
            af_evs = {}
            o_mm_evs = {}

            def wo_load(gidx, waits=()):
                fc = gidx % nfc
                return p.ev("sync", lambda e, fc=fc, gidx=gidx: e.dma_start(
                    out=A(AT.wo2 + (gidx % 2) * c.D, c.D),
                    in_=WoT[fc * 128:(fc + 1) * 128, :]),
                    waits=list(waits), sem=sWO[gidx % 2], dma=True)

            def af_load(gidx, b, waits=()):
                ttb = gidx // nfc
                fc = gidx % nfc
                return p.ev("gpsimd",
                            lambda e, ttb=ttb, fc=fc, gidx=gidx, b=b:
                            e.dma_start(
                                out=A(AT.att_fc + (gidx % 2) * c.TB, c.TB),
                                in_=attnT_dram[b, fc * 128:(fc + 1) * 128,
                                               ttb * c.TB:(ttb + 1) * c.TB]),
                            waits=list(waits), sem=sAF[gidx % 2],
                            dma=True)

            xt_evs = {}

            def xt_load(ttb, tq, b, waits=()):
                tok0 = b * c.T + ttb * c.TB + tq * 128
                return p.ev("gpsimd",
                            lambda e, tok0=tok0, tq=tq: e.dma_start(
                                out=A(AT.xtile + tq * c.D, c.D),
                                in_=x_j[tok0:tok0 + 128, :]),
                            waits=list(waits), sem=sXT, dma=True)

            o_evac_ev = {}
            def st_gc(fc, h2, g):
                return ((b * c.HPAIR + fc) * 2 + h2) * c.NG + g

            for ttb in range(c.T // c.TB):
                if ttb == 0:
                    ph1g = [] if b > 0 else (
                        [ph1_fin["s_end"], txe_last_ev[c.NT - 1]] +
                        ph1_fin["sdone"])
                    wo_evs[0] = wo_load(0, waits=att_done)
                    af_evs[0] = af_load(0, b, waits=att_done)
                    wo_evs[1] = wo_load(1)
                    af_evs[1] = af_load(1, b)
                    for tq in range(c.TBT):
                        xt_evs[tq] = xt_load(
                            0, tq, b,
                            waits=(ph1g if tq == 0 and b == 0 else
                                   (list(o_stores[b - 1])
                                    if tq == 0 and b > 0 else ())))
                    xt_all = sXT.ev()
                for fc in range(nfc):
                    gidx = ttb * nfc + fc
                    w0 = [wo_evs[gidx], af_evs[gidx]]
                    if fc == 0:
                        if ttb > 0:
                            w0.append(o_evac_ev[ttb - 1])
                        else:
                            w0 += [bk_wait(i) for i in range(8)]
                    lastmm = None
                    ntl = c.TBT * c.NDW
                    for tq in range(c.TBT):
                        for dc2 in range(c.NDW):
                            tile = tq * c.NDW + dc2
                            last = (tile == ntl - 1)
                            if not last:
                                p.do("tensor", mm(
                                    bank(tile, c.DW),
                                    A(AT.att_fc + (gidx % 2) * c.TB,
                                      c.TB)[:, tq * 128:(tq + 1) * 128],
                                    A(AT.wo2 + (gidx % 2) * c.D,
                                      c.D)[:, dc2 * c.DW:(dc2 + 1) * c.DW],
                                    fc == 0, False),
                                    waits=w0 if tile == 0 else ())
                            else:
                                lastmm = p.ev("tensor", mm(
                                    bank(tile, c.DW),
                                    A(AT.att_fc + (gidx % 2) * c.TB,
                                      c.TB)[:, tq * 128:(tq + 1) * 128],
                                    A(AT.wo2 + (gidx % 2) * c.D,
                                      c.D)[:, dc2 * c.DW:(dc2 + 1) * c.DW],
                                    fc == 0, fc == nfc - 1), sem=sT)
                    o_mm_evs[gidx] = lastmm
                    if gidx + 2 < NGR:
                        nfcx = (gidx + 2) % nfc
                        nttb = (gidx + 2) // nfc
                        wo_evs[gidx + 2] = wo_load(gidx + 2,
                                                   waits=[lastmm])
                        af_evs[gidx + 2] = af_load(
                            gidx + 2, b,
                            waits=[lastmm,
                                   attst_store_evs[st_gc(nfcx, 0, nttb)],
                                   attst_store_evs[st_gc(nfcx, 1, nttb)]])
                oend = o_mm_evs[ttb * nfc + nfc - 1]
                last_evac = None
                if ttb == 0 and b == 0:
                    ost_prev = {}
                for tq in range(c.TBT):
                    for dc2 in range(c.NDW):
                        tile = tq * c.NDW + dc2
                        ow = [oend, xt_all]
                        if b == 0 and ttb == 0 and tile == 0:
                            ow += ph1_fin["tdone"]
                        p.do("vector", lambda e, tile=tile, dc2=dc2:
                             e.tensor_tensor(
                                 bank(tile, c.DW), bank(tile, c.DW),
                                 cs_boB[:, dc2 * c.DW:(dc2 + 1) * c.DW],
                                 ALU.add), waits=ow)
                        ove = p.ev("vector",
                                   lambda e, tile=tile, tq=tq, dc2=dc2:
                                   e.tensor_tensor(
                                       A(AT.ost + (tile % 2) * c.DW, c.DW),
                                       bank(tile, c.DW),
                                       A(AT.xtile + tq * c.D,
                                         c.D)[:, dc2 * c.DW:
                                              (dc2 + 1) * c.DW],
                                       ALU.add),
                                   waits=[ost_prev.get(tile % 2)],
                                   sem=sV, drain=True)
                        last_evac = ove
                        bk_set(tile, ove)
                        tok0 = b * c.T + ttb * c.TB + tq * 128
                        ost_prev[tile % 2] = p.ev(
                            "sync", lambda e, tile=tile, tok0=tok0,
                            dc2=dc2: e.dma_start(
                                out=attres_dram[
                                    tok0:tok0 + 128,
                                    dc2 * c.DW:(dc2 + 1) * c.DW],
                                in_=A(AT.ost + (tile % 2) * c.DW, c.DW)),
                            waits=[ove], sem=sOS[tile % 2], dma=True)
                o_evac_ev[ttb] = last_evac
                if ttb + 1 < c.T // c.TB:
                    for tq in range(c.TBT):
                        xt_evs[tq] = xt_load(ttb + 1, tq, b,
                                             waits=[last_evac]
                                             if tq == 0 else ())
                    xt_all = sXT.ev()
            o_stores[b] = [sOS[0].ev(), sOS[1].ev()]

            af_done[b] = (sAF[0].ev(), sAF[1].ev(),
                          sWO[0].ev(), sWO[1].ev())
        attn_v_last = sV.ev()
        if c.debug:
            p.ev("sync", lambda e: e.dma_start(
                out=dbg["attres"][:, :], in_=attres_dram[:, :]),
                waits=list(o_stores[c.BL - 1]), sem=sDBG, dma=True)
        if stop_after == "attn":
            routing_done[0] = routing_done[0] or emit_routing()
            _finish(p, all_sems)
            _emit(nc, p)
            return nc

        ar_id = routing_done[0]

        idraw = A(MO.idraw, 128)
        idcol = A(MO.idcol, 2 * CAPT)
        idu = A(MO.idu, 2 * CAPT, U32)
        itev = []
        for ch in range(2):
            gie = p.ev("gpsimd", lambda e, ch=ch: e.indirect_dma_start(
                out=idraw[:], out_offset=None,
                in_=idtab_full.rearrange("(a b) -> a b", b=128),
                in_offset=IndirectOffsetOnAxis(cs_idxb[:, ch:ch + 1], 0)),
                waits=[ar_id] + list(o_stores[c.BL - 1]) if ch == 0
                      else [itev[0][0]],
                sem=sID, dma=True)
            ite = p.ev("tensor", tpose(bank(2)[:, 0:128], idraw[:],
                                       cs_ident[:], True, True),
                       waits=[gie, bk_wait(2)], sem=sT)
            cv = p.ev("vector", lambda e, ch=ch: e.tensor_copy(
                idcol[:, ch * CAPT:(ch + 1) * CAPT], bank(2)[:, 0:CAPT]),
                waits=[ite], sem=sV)
            bk_set(2, cv)
            itev.append((ite, cv))
        idu_ev = p.ev("vector", lambda e: e.tensor_copy(idu[:], idcol[:]),
                      waits=[itev[-1][1]], sem=sV, drain=True)
        b2ld = p.ev("sync", lambda e: e.dma_start(
            out=A(MO.b2B, c.D), in_=b2B[:, :]),
            waits=list(o_stores[c.BL - 1]), sem=sC, dma=True)

        ag_xn_ev = sAGX.ev()
        hmidT = A(MO.hmidT, c.FFC * c.BS)
        xbst = MO.xbst
        gth_evs = {}
        str_evs = {}

        def gather(stile):
            par = stile % 6
            w = [idu_ev, ag_xn_ev] if stile == 0 else []
            if stile >= 6:
                w.append(str_evs[stile - 6])
            gth_evs[stile] = p.ev(
                "gpsimd", lambda e, stile=stile, par=par:
                e.indirect_dma_start(
                    out=A(xbst + par * c.D, c.D),
                    out_offset=None, in_=xn_full[:, :],
                    in_offset=IndirectOffsetOnAxis(
                        idu[:, stile:stile + 1], 0)),
                waits=w, sem=sGA[stile % 6], dma=True)

        nst_tot = c.NSB * c.BST
        for stile in range(min(6, nst_tot)):
            gather(stile)

        w1_evs = {}
        w2_evs = {}

        def w1_load(ffc, waits=()):
            w1_evs[ffc] = p.ev(
                "sync", lambda e, ffc=ffc: e.dma_start(
                    out=A(MO.w1 + (ffc % 4) * c.D, c.D), in_=W1H[ffc]),
                waits=list(waits), sem=sW1[ffc % 4], dma=True)

        def w2_load(ffc, waits=()):
            w2_evs[ffc] = p.ev(
                "scalar", lambda e, ffc=ffc: e.dma_start(
                    out=A(MO.w2 + (ffc % 4) * c.D, c.D),
                    in_=W2_j[ffc * 128:(ffc + 1) * 128, :]),
                waits=list(waits), sem=sW2[ffc % 4], dma=True)

        moe_pre = (list(o_stores[c.BL - 1]) +
                   [sAS[0].ev(), sAS[1].ev()])
        for ffc in range(min(4, c.FFC)):
            w1_load(ffc, waits=moe_pre if ffc == 0 else ())
            w2_load(ffc, waits=moe_pre if ffc == 0 else ())

        w1mm_last = {}
        yb_evac_last = {}
        yst_evs = []
        BSC = 512
        nybc = c.NSLOT // BSC
        ags_done = 0
        for sblk in range(c.NSB):
            xpar = sblk % 2
            xbT = A(MO.xbT + xpar * c.DC * c.BS, c.DC * c.BS)
            xb3 = xbT.rearrange("p (dcv w) -> p dcv w", dcv=c.DC)
            tr_evs = []
            g2 = 0
            for i in range(c.BST):
                stile = sblk * c.BST + i
                for g in range(_cdiv(c.DC, 4)):
                    n = min(4, c.DC - g * 4)
                    tbk = 2 + g2 % 2
                    w0 = [gth_evs[stile]] if g == 0 else []
                    if g2 >= 2:
                        w0.append(tr_evs[g2 - 2])
                    else:
                        w0.append(bk_wait(tbk))
                    if sblk >= 2 and g == 0 and i == 0:
                        w0.append((sT, w1mm_last[sblk - 2]))
                    lastt = None
                    for k in range(n):
                        dc = g * 4 + k
                        if k < n - 1:
                            p.do("tensor", tpose(
                                bank(tbk)[:, k * 128:(k + 1) * 128],
                                A(xbst + (stile % 6) * c.D,
                                  c.D)[:, dc * 128:(dc + 1) * 128],
                                cs_ident[:], k == 0, False),
                                waits=w0 if k == 0 else ())
                        else:
                            lastt = p.ev("tensor", tpose(
                                bank(tbk)[:, k * 128:(k + 1) * 128],
                                A(xbst + (stile % 6) * c.D,
                                  c.D)[:, dc * 128:(dc + 1) * 128],
                                cs_ident[:], k == 0, True),
                                waits=w0 if n == 1 else (), sem=sT)
                    tr_evs.append(p.ev(
                        "vector", lambda e, g=g, n=n, i=i, xb3=xb3, tbk=tbk:
                        e.tensor_copy(
                            xb3[:, g * 4:g * 4 + n,
                                i * 128:(i + 1) * 128],
                            bank(tbk)[:, :n * 128].rearrange(
                                "p (a w) -> p a w", a=n)),
                        waits=[lastt], sem=sV))
                    g2 += 1
                str_evs[stile] = lastt
            xbT_done = tr_evs[-1]
            bk_set(2, tr_evs[-1] if g2 % 2 == 1 else tr_evs[-2])
            bk_set(3, tr_evs[-2] if g2 % 2 == 1 else tr_evs[-1])
            for i in range(c.BST):
                stile = (sblk + 1) * c.BST + i
                if stile < nst_tot and stile not in gth_evs:
                    gather(stile)
            relu_evs = []
            for ffc in range(c.FFC):
                bk = ffc % 2
                w0 = [w1_evs[ffc], xbT_done]
                if ffc >= 2:
                    w0.append(relu_evs[ffc - 2])
                else:
                    w0.append(bk_wait(bk))
                wb = A(MO.w1 + (ffc % 4) * c.D, c.D)
                lastmm = None
                for dc in range(c.DC):
                    if dc < c.DC - 1:
                        p.do("tensor", mm(
                            bank(bk, c.BS), wb[:, dc * 128:(dc + 1) * 128],
                            xbT[:, dc * c.BS:(dc + 1) * c.BS],
                            dc == 0, False), waits=w0 if dc == 0 else ())
                    else:
                        lastmm = p.ev("tensor", mm(
                            bank(bk, c.BS), wb[:, dc * 128:(dc + 1) * 128],
                            xbT[:, dc * c.BS:(dc + 1) * c.BS],
                            dc == 0, True),
                            waits=w0 if c.DC == 1 else (), sem=sT)
                relu_evs.append(p.ev(
                    "scalar", lambda e, ffc=ffc, bk=bk: e.activation(
                        out=hmidT[:, ffc * c.BS:(ffc + 1) * c.BS],
                        in_=bank(bk, c.BS), func=AF.Relu,
                        bias=cs_b1[:, ffc:ffc + 1]),
                    waits=[lastmm], sem=sS))
                bk_set(bk, relu_evs[-1])
                nxt = ffc + 4
                if nxt < c.FFC:
                    w1_load(nxt, waits=[lastmm])
                elif sblk + 1 < c.NSB:
                    w1_load(nxt - c.FFC, waits=[lastmm])
            w1mm_last[sblk] = sT.n
            ntl = c.BST * c.NDW
            w2_lastmm = None
            for ffc in range(c.FFC):
                wb = A(MO.w2 + (ffc % 4) * c.D, c.D)
                w0 = [w2_evs[ffc]]
                if ffc == 0:
                    w0.append(relu_evs[-1])
                    w0.append(b2ld)
                    w0 += [bk_wait(i) for i in range(2, 8)]
                lastmm = None
                for tq in range(c.BST):
                    for dc2 in range(c.NDW):
                        tile = tq * c.NDW + dc2
                        last = tile == ntl - 1
                        if not last:
                            p.do("tensor", mm(
                                bank(tile, c.DW),
                                hmidT[:, ffc * c.BS + tq * 128:
                                      ffc * c.BS + (tq + 1) * 128],
                                wb[:, dc2 * c.DW:(dc2 + 1) * c.DW],
                                ffc == 0, False),
                                waits=w0 if tile == 0 else ())
                        else:
                            lastmm = p.ev("tensor", mm(
                                bank(tile, c.DW),
                                hmidT[:, ffc * c.BS + tq * 128:
                                      ffc * c.BS + (tq + 1) * 128],
                                wb[:, dc2 * c.DW:(dc2 + 1) * c.DW],
                                ffc == 0, ffc == c.FFC - 1), sem=sT)
                nxt = ffc + 4
                if nxt < c.FFC:
                    w2_load(nxt, waits=[lastmm])
                elif sblk + 1 < c.NSB:
                    w2_load(nxt - c.FFC, waits=[lastmm])
                w2_lastmm = lastmm
            last_ev = None
            for tq in range(c.BST):
                for dc2 in range(c.NDW):
                    tile = tq * c.NDW + dc2
                    ow = [w2_lastmm]
                    if dc2 == 0:
                        if tq >= 2:
                            ow.append((sYS[sblk], 16 * tq))
                        elif sblk > 0:
                            ow.append((sYS[sblk - 1], 16 * c.BST))
                    if sblk == 0 and tile == 0:
                        ow += list(o_stores[c.BL - 1])
                    ye = p.ev("vector", lambda e, tile=tile, tq=tq,
                              dc2=dc2: e.tensor_tensor(
                                  A(MO.ybst + (tq % 2) * c.D,
                                    c.D)[:, dc2 * c.DW:(dc2 + 1) * c.DW],
                                  bank(tile, c.DW),
                                  A(MO.b2B,
                                    c.D)[:, dc2 * c.DW:(dc2 + 1) * c.DW],
                                  ALU.add), waits=ow, sem=sV)
                    bk_set(tile, ye)
                yve = p.ev("vector", lambda e: e.drain(), sem=sV)
                last_ev = yve
                yst_evs.append(p.ev(
                    "sync", lambda e, sblk=sblk, tq=tq: e.dma_start(
                        out=yb_loc[sblk * c.BS + tq * 128:
                                   sblk * c.BS + (tq + 1) * 128, :],
                        in_=A(MO.ybst + (tq % 2) * c.D, c.D)),
                    waits=[yve], sem=sYS[sblk], dma=True))
            yb_evac_last[sblk] = last_ev
            p.ev("gpsimd", lambda e, sblk=sblk: e.collective_compute(
                "AllGather", ALU.bypass,
                replica_groups=[list(range(c.n_cores))],
                ins=[yb_loc[sblk * c.BS:(sblk + 1) * c.BS, :]],
                outs=[yb_full[sblk * c.n_cores * c.BS:
                              (sblk + 1) * c.n_cores * c.BS, :]]),
                waits=[(sYS[sblk], 16 * c.BST)], sem=sYAG)
        if c.debug:
            p.ev("sync", lambda e: e.dma_start(
                out=dbg["ybloc"][:, :], in_=yb_loc[:, :]),
                waits=[(sYS[c.NSB - 1], 16 * c.BST)], sem=sDBG, dma=True)

        yag_all = sYAG.ev()
        moe_v_last = sV.ev()
        ar_evs = {}
        g_evs = {}
        comb_v = {}

        def comb_loads(tt):
            par = tt % 4
            if tt < 4:
                w0 = list(o_stores[c.BL - 1]) + [yag_all, moe_v_last]
                gw = [yag_all, moe_v_last]
            else:
                w0 = [comb_v[tt - 4]]
                gw = [comb_v[tt - 4]]
            ar_evs[tt] = p.ev("sync", lambda e, tt=tt, par=par: e.dma_start(
                out=A(MO.arb + par * c.D, c.D),
                in_=attres_dram[tt * 128:(tt + 1) * 128, :]),
                waits=w0, sem=sAR2[tt % 4], dma=True)
            p.ev("gpsimd", lambda e, tt=tt, par=par: e.indirect_dma_start(
                out=A(MO.g0 + par * c.D, c.D), out_offset=None,
                in_=yb_full[:, :],
                in_offset=IndirectOffsetOnAxis(
                    st_row[:, 2 * tt:2 * tt + 1], 0)),
                waits=gw, sem=sGC[tt % 4], dma=True)
            g_evs[tt] = p.ev("gpsimd", lambda e, tt=tt, par=par:
                             e.indirect_dma_start(
                                 out=A(MO.g1 + par * c.D, c.D),
                                 out_offset=None, in_=yb_full[:, :],
                                 in_offset=IndirectOffsetOnAxis(
                                     st_row[:, 2 * tt + 1:2 * tt + 2], 0)),
                             waits=gw, sem=sGC[tt % 4], dma=True)

        y_prev = {}
        for tt in range(min(4, c.NT)):
            comb_loads(tt)
        for tt in range(c.NT):
            par = tt % 4
            p.do("vector", lambda e, tt=tt, par=par: e.tensor_scalar(
                A(MO.g0 + par * c.D, c.D), A(MO.g0 + par * c.D, c.D),
                st_w[:, 2 * tt:2 * tt + 1], None, ALU.mult),
                waits=[g_evs[tt], ar_evs[tt]])
            p.do("vector", lambda e, tt=tt, par=par: e.scalar_tensor_tensor(
                A(MO.g1 + par * c.D, c.D), A(MO.g1 + par * c.D, c.D),
                st_w[:, 2 * tt + 1:2 * tt + 2], A(MO.g0 + par * c.D, c.D),
                ALU.mult, ALU.add), drain=True)
            ove = p.ev("vector", lambda e, tt=tt, par=par: e.tensor_tensor(
                A(MO.outst + par * c.D, c.D), A(MO.arb + par * c.D, c.D),
                A(MO.g1 + par * c.D, c.D), ALU.add),
                waits=[y_prev.get(tt % 4)], sem=sV, drain=True)
            comb_v[tt] = ove
            y_prev[tt % 4] = p.ev(
                "sync", lambda e, tt=tt, par=par: e.dma_start(
                    out=y_j[tt * 128:(tt + 1) * 128, :],
                    in_=A(MO.outst + par * c.D, c.D)), waits=[ove],
                sem=sY[tt % 4], dma=True)
            if tt + 4 < c.NT:
                comb_loads(tt + 4)
        _finish(p, all_sems)
        _emit(nc, p)
    finally:
        for cm in reversed(cms):
            cm.__exit__(None, None, None)
    return nc


def _finish(p, all_sems):
    evs = [s.ev() for s in all_sems]

    def thunk(e, evs=tuple(evs)):
        for s, v in evs:
            if v > 0:
                e.wait_ge(s.h, v)
    p.q["gpsimd"].append(thunk)


def _emit(nc, p):
    with nc.Block() as block:
        @block.sync
        def _(e):
            for f in p.q["sync"]:
                f(e)

        @block.tensor
        def _(e):
            for f in p.q["tensor"]:
                f(e)

        @block.vector
        def _(e):
            for f in p.q["vector"]:
                f(e)

        @block.scalar
        def _(e):
            for f in p.q["scalar"]:
                f(e)

        @block.gpsimd
        def _(e):
            for f in p.q["gpsimd"]:
                f(e)



def kernel(**inputs):
    cfg = CFG()
    ren = {"L0_attn_x": "x", "L0_attn_Wq": "Wq", "L0_attn_bq": "bq",
           "L0_attn_Wk": "Wk", "L0_attn_bk": "bk", "L0_attn_Wv": "Wv",
           "L0_attn_bv": "bv", "L0_attn_Wo": "Wo", "L0_attn_bo": "bo",
           "L0_ffn_gate_w": "gate_w", "L0_ffn_gate_b": "gate_b",
           "L0_ffn_W1": "W1", "L0_ffn_b1": "b1", "L0_ffn_W2": "W2",
           "L0_ffn_b2": "b2"}
    inp = {ren[k]: np.asarray(v) for k, v in inputs.items()}
    nc = build_program(cfg)
    in_maps = [host_prep(cfg, inp, j) for j in range(cfg.n_cores)]
    res = run_bass_kernel_spmd(nc, in_maps, core_ids=list(range(cfg.n_cores)))
    out = np.concatenate([r["y_j"].reshape(cfg.BL, cfg.T, cfg.D)
                          for r in res.results], axis=0)
    return np.ascontiguousarray(out).astype(np.float32)


# version 21 (final)
# speedup vs baseline: 1.0091x; 1.0091x over previous
import numpy as np

import concourse.bass as bass
import concourse.mybir as mybir
from concourse.bass import IndirectOffsetOnAxis
from concourse.bass_utils import run_bass_kernel_spmd

F32 = mybir.dt.float32
U32 = mybir.dt.uint32
AF = mybir.ActivationFunctionType
ALU = mybir.AluOpType
AX = mybir.AxisListType

NEG = -1.0e30
ENGINES = ("sync", "tensor", "vector", "scalar", "gpsimd")


class CFG:
    def __init__(self, n_cores=8, B=16, T=2048, D=1024, H=16, FF=4096, E=8,
                 CAP=2560, eps=1e-5, rope_base=10000.0, debug=False,
                 simtr=False):
        self.n_cores = n_cores
        self.B, self.T, self.D, self.H, self.FF, self.E = B, T, D, H, FF, E
        self.CAP, self.eps, self.rope_base = CAP, eps, rope_base
        self.debug = debug
        self.simtr = simtr
        self.HD = D // H
        self.BL = B // n_cores
        self.NTOK = self.BL * T
        self.NT = self.NTOK // 128
        self.DC = D // 128
        self.FFC = FF // 128
        self.TKC = T // 128
        self.SW = min(512, T)
        self.NV = self.SW // 128
        self.NG = self.TKC // self.NV
        self.HPAIR = H // 2
        self.TTOT = B * T
        self.NSLOT = 2 * CAP
        self.DW = min(512, D)
        self.NDW = D // self.DW
        self.TB = min(512, T)
        self.TBT = self.TB // 128
        self.BS = min(512, self.NSLOT)
        self.BST = self.BS // 128
        self.NSB = self.NSLOT // self.BS
        assert self.HD == 64 and D % 128 == 0 and T % 128 == 0
        assert CAP % 128 == 0 and E == n_cores and B % n_cores == 0
        assert self.TKC % self.NV == 0 and self.NSLOT % self.BS == 0


class Sem:
    __slots__ = ("h", "n", "name")

    def __init__(self, handle, name):
        self.h = handle
        self.n = 0
        self.name = name

    def ev(self):
        return (self, self.n)


class Sched:

    def __init__(self, nc):
        self.nc = nc
        self.q = {e: [] for e in ENGINES}

    def do(self, eng, fn=None, waits=(), inc=None, drain=False):
        for w in waits:
            assert w is None or (isinstance(w, tuple) and
                                 hasattr(w[0], "h")), ("bad wait", w)
        waits = tuple(w for w in waits if w is not None and w[1] > 0)

        def thunk(e, fn=fn, waits=waits, inc=inc, drain=drain):
            if drain:
                e.drain()
            for s, v in waits:
                e.wait_ge(s.h, v)
            r = fn(e) if fn is not None else None
            if inc is not None:
                if r is not None:
                    r.then_inc(inc[0].h, inc[1])
                else:
                    e.sem_inc(inc[0].h, inc[1])

        self.q[eng].append(thunk)

    def ev(self, eng, fn, waits=(), sem=None, dma=False, drain=False):
        amt = 16 if dma else 1
        sem.n += amt
        self.do(eng, fn, waits=waits, inc=(sem, amt), drain=drain)
        return (sem, sem.n)


def _cdiv(a, b):
    return (a + b - 1) // b



def host_prep(cfg, inp, core):
    j = core
    c = cfg
    d = {}
    x = np.ascontiguousarray(inp["x"].reshape(c.B, c.T, c.D))
    d["x_j"] = np.ascontiguousarray(
        x[j * c.BL:(j + 1) * c.BL].reshape(c.NTOK, c.D))
    for nm in ("Wq", "Wk", "Wv"):
        wt = np.ascontiguousarray(inp[nm].T)
        d[nm + "H"] = np.ascontiguousarray(
            wt.reshape(c.DC, 128, c.HPAIR, 128).transpose(2, 1, 0, 3)
            .reshape(c.HPAIR, 128, c.D))
    d["WoT"] = np.ascontiguousarray(inp["Wo"].T)
    for nm in ("bq", "bk"):
        d[nm + "P"] = np.ascontiguousarray(inp[nm].reshape(c.DC, 128).T)
    d["bvB"] = np.ascontiguousarray(
        np.broadcast_to(inp["bv"].reshape(1, c.D), (128, c.D)))
    d["boB"] = np.ascontiguousarray(
        np.broadcast_to(inp["bo"].reshape(1, c.D), (128, c.D)))
    d["gate_wT"] = np.ascontiguousarray(
        inp["gate_w"].T.reshape(c.DC, 128, c.E).transpose(1, 0, 2)
        .reshape(128, c.DC * c.E))
    d["gate_bR"] = np.ascontiguousarray(inp["gate_b"].reshape(1, c.E))
    d["W1H"] = np.ascontiguousarray(
        inp["W1"][j].reshape(c.DC, 128, c.FFC, 128).transpose(2, 1, 0, 3)
        .reshape(c.FFC, 128, c.D))
    d["b1P"] = np.ascontiguousarray(inp["b1"][j].reshape(c.FFC, 128).T)
    d["W2_j"] = np.ascontiguousarray(inp["W2"][j])
    d["b2B"] = np.ascontiguousarray(
        np.broadcast_to(inp["b2"][j].reshape(1, c.D), (128, c.D)))

    hd = c.HD
    half = hd // 2
    theta = (1.0 / c.rope_base **
             (np.arange(0, hd, 2, dtype=np.float32) / np.float32(hd)))
    pos = np.arange(1, c.T + 1, dtype=np.float32)
    ang = pos[:, None] * theta[None, :]
    ang = np.concatenate([ang, ang], axis=1)
    cos = np.cos(ang).T.astype(np.float32)
    sin = np.sin(ang).T.astype(np.float32)
    sgn = np.ones((hd, 1), np.float32)
    sgn[:half] = -1.0
    sinS = (sin * sgn).astype(np.float32)
    d["ropeCos2"] = np.ascontiguousarray(np.concatenate([cos, cos], axis=0))
    d["ropeSin2"] = np.ascontiguousarray(np.concatenate([sinS, sinS], axis=0))
    R = np.zeros((128, 128), np.float32)
    for blk in range(2):
        for fo in range(hd):
            fi = fo + half if fo < half else fo - half
            R[blk * hd + fi, blk * hd + fo] = 1.0
    d["ropeR"] = R

    f = np.arange(c.SW)[None, :]
    pp = np.arange(128)[:, None]
    d["maskdiag"] = np.ascontiguousarray(np.stack(
        [np.where(f <= pp + 128 * v, 0.0, NEG).astype(np.float32)
         for v in range(c.NV)]))
    d["ident128"] = np.eye(128, dtype=np.float32)
    d["ones128"] = np.ones((1, 128), np.float32)
    d["ones_col"] = np.ones((128, 1), np.float32)
    d["ltri"] = np.ascontiguousarray(
        np.tril(np.ones((128, 128), np.float32), -1).T)
    d["iota8"] = np.ascontiguousarray(
        np.broadcast_to(np.arange(c.E, dtype=np.float32), (128, c.E)).copy())
    d["prefmask"] = np.ascontiguousarray(
        (np.arange(c.n_cores) < j).astype(np.float32).reshape(c.n_cores, 1))
    w = c.CAP // 128
    base = j * w + np.minimum(np.arange(128), w - 1)
    nrows = c.E * c.CAP // 128
    d["idxbase"] = np.stack([base, base + nrows], axis=1).astype(np.uint32)
    d["tokid0"] = (j * c.NTOK + np.arange(128, dtype=np.float32)
                   ).reshape(128, 1)
    d["dropidx"] = (2 * c.E * c.CAP + np.arange(128, dtype=np.float32)
                    ).reshape(128, 1)
    return d



def build_program(cfg, stop_after=None):
    c = cfg
    nc = bass.Bass("TRN2", target_bir_lowering=False)

    def din(name, shape, dtype=F32):
        return nc.dram_tensor(name, list(shape), dtype,
                              kind="ExternalInput").ap()

    def dout(name, shape, dtype=F32):
        return nc.dram_tensor(name, list(shape), dtype,
                              kind="ExternalOutput").ap()

    def dtmp(name, shape, dtype=F32):
        return nc.dram_tensor(name, list(shape), dtype, kind="Internal").ap()

    x_j = din("x_j", [c.NTOK, c.D])
    WqH = din("WqH", [c.HPAIR, 128, c.D])
    WkH = din("WkH", [c.HPAIR, 128, c.D])
    WvH = din("WvH", [c.HPAIR, 128, c.D])
    WoT = din("WoT", [c.D, c.D])
    bqP = din("bqP", [128, c.DC])
    bkP = din("bkP", [128, c.DC])
    bvB = din("bvB", [128, c.D])
    boB = din("boB", [128, c.D])
    gate_wT = din("gate_wT", [128, c.DC * c.E])
    gate_bR = din("gate_bR", [1, c.E])
    W1H = din("W1H", [c.FFC, 128, c.D])
    b1P = din("b1P", [128, c.FFC])
    W2_j = din("W2_j", [c.FF, c.D])
    b2B = din("b2B", [128, c.D])
    ropeCos2 = din("ropeCos2", [128, c.T])
    ropeSin2 = din("ropeSin2", [128, c.T])
    ropeR = din("ropeR", [128, 128])
    maskdiag = din("maskdiag", [c.NV, 128, c.SW])
    ident128 = din("ident128", [128, 128])
    ones128 = din("ones128", [1, 128])
    ones_col = din("ones_col", [128, 1])
    ltri = din("ltri", [128, 128])
    iota8 = din("iota8", [128, c.E])
    prefmask = din("prefmask", [c.n_cores, 1])
    idxbase = din("idxbase", [128, 2], U32)
    tokid0 = din("tokid0", [128, 1])
    dropidx = din("dropidx", [128, 1])

    y_j = dout("y_j", [c.NTOK, c.D])

    xn_dram = dtmp("xn_dram", [c.NTOK, c.D])
    xnT_dram = dtmp("xnT_dram", [c.D, c.NTOK])
    attres_dram = dtmp("attres_dram", [c.NTOK, c.D])
    attnT_dram = dtmp("attnT_dram", [c.BL, c.D, c.T])
    cnt_in = dtmp("cnt_in", [1, 2 * c.E])
    cnt_out = dtmp("cnt_out", [c.n_cores, 2 * c.E])
    idtab_loc = dtmp("idtab_loc", [2 * c.E * c.CAP + 128])
    idtab_full = dtmp("idtab_full", [2 * c.E * c.CAP + 128])
    xn_full = dtmp("xn_full", [c.TTOT, c.D])
    yb_loc = dtmp("yb_loc", [c.NSLOT, c.D])
    yb_full = dtmp("yb_full", [c.n_cores * c.NSLOT, c.D])

    dbg = {}
    if c.debug:
        dbg["xn"] = dout("dbg_xn", [c.NTOK, c.D])
        dbg["xnT"] = dout("dbg_xnT", [c.D, c.NTOK])
        dbg["route"] = dout("dbg_route", [c.NT, 128, 10])
        dbg["cnt"] = dout("dbg_cnt", [c.n_cores, 2 * c.E])
        dbg["idtab"] = dout("dbg_idtab", [2, c.E * c.CAP])
        dbg["attres"] = dout("dbg_attres", [c.NTOK, c.D])
        dbg["ybloc"] = dout("dbg_ybloc", [c.NSLOT, c.D])

    cms = []

    def enter(cm):
        v = cm.__enter__()
        cms.append(cm)
        return v

    def sb(name, shape, dtype=F32):
        return enter(nc.sbuf_tensor(name, list(shape), dtype))

    def mksem(name):
        return Sem(enter(nc.semaphore(name)), name)

    try:
        sT, sV, sS, sG = (mksem(n) for n in ("sT", "sV", "sS", "sG"))
        sCst = mksem("sCst")
        sIZ = mksem("sIZ")
        sC = mksem("sC")
        sDBG = mksem("sDBG")
        sXL = [mksem("sXL0"), mksem("sXL1")]
        sXN = [mksem("sXN0"), mksem("sXN1")]
        sXNT = [mksem("sXNT%d" % i) for i in range(c.BL)]
        sAGC = mksem("sAGC")
        sAGX = mksem("sAGX")
        sARI = mksem("sARI")
        sSCT = mksem("sSCT")
        sWS = [mksem("sWS%d" % i) for i in range(4)]
        sXB = [mksem("sXB%d" % i) for i in range(4)]
        sWO = [mksem("sWO%d" % i) for i in range(2)]
        sAF = [mksem("sAF%d" % i) for i in range(2)]
        sXT = mksem("sXT")
        sAS = [mksem("sAS%d" % i) for i in range(2)]
        sOS = [mksem("sOS%d" % i) for i in range(2)]
        sW1 = [mksem("sW1%d" % i) for i in range(4)]
        sW2 = [mksem("sW2%d" % i) for i in range(4)]
        sID = mksem("sID")
        sGA = [mksem("sGA%d" % i) for i in range(6)]
        sYS = [mksem("sYS%d" % i) for i in range(c.NSB)]
        sYAG = mksem("sYAG")
        sAR2 = [mksem("sAR2%d" % i) for i in range(4)]
        sGC = [mksem("sGC%d" % i) for i in range(4)]
        sY = [mksem("sY%d" % i) for i in range(4)]
        all_sems = ([sT, sV, sS, sG, sCst, sIZ, sC, sDBG, sAGC, sAGX, sARI,
                     sSCT, sXT, sID, sYAG] + sXB + sXL + sXN + sXNT + sWS +
                    sWO + sAF + sAS + sOS + sW1 + sW2 + sGA + sYS + sAR2 +
                    sGC + sY)

        p = Sched(nc)

        def mm(out, lhsT, rhs, start, stop):
            return lambda e: e.matmul(out, lhsT, rhs, start=start, stop=stop,
                                      skip_group_check=True)

        def tpose(out, in_, ident, start, stop):
            return lambda e: e.matmul(out, in_, ident, start=start, stop=stop,
                                      is_transpose=True, skip_group_check=True)

        cs_mask = sb("cs_mask", [128, c.NV * c.SW])
        cs_ident = sb("cs_ident", [128, 128])
        cs_ones = sb("cs_ones", [1, 128])
        cs_onesc = sb("cs_onesc", [128, 1])
        cs_ltri = sb("cs_ltri", [128, 128])
        cs_iota8 = sb("cs_iota8", [128, c.E])
        cs_pref = sb("cs_pref", [c.n_cores, 1])
        cs_idxb = sb("cs_idxb", [128, 2], U32)
        cs_tok0 = sb("cs_tok0", [128, 1])
        cs_drop = sb("cs_drop", [128, 1])
        cs_cos = sb("cs_cos", [128, c.T])
        cs_sin = sb("cs_sin", [128, c.T])
        cs_R = sb("cs_R", [128, 128])
        cs_bq = sb("cs_bq", [128, c.DC])
        cs_bk = sb("cs_bk", [128, c.DC])
        cs_bvB = sb("cs_bvB", [128, c.D])
        cs_boB = sb("cs_boB", [128, c.D])
        cs_gw = sb("cs_gw", [128, c.DC * c.E])
        cs_gb = sb("cs_gb", [1, c.E])
        cs_b1 = sb("cs_b1", [128, c.FFC])

        st_p = sb("st_p", [128, 2 * c.NT])
        st_e = sb("st_e", [128, 2 * c.NT])
        st_pos = sb("st_pos", [128, 2 * c.NT])
        st_w = sb("st_w", [128, 2 * c.NT])
        st_row = sb("st_row", [128, 2 * c.NT], U32)
        st_oh = sb("st_oh", [128, c.NT * 2 * c.E])
        cumprev = sb("cumprev", [1, 2 * c.E])
        gstage = sb("gstage", [128, 56])
        offsb = sb("offsb", [128, 2 * c.E])
        statb = sb("statb", [128, 8])
        allcnt = sb("allcnt", [c.n_cores, 2 * c.E])

        pb = enter(nc.psum_tensor("pb", [128, 4096], F32))

        def bank(i, w=None):
            w = 512 if w is None else w
            return pb[:, i * 512:i * 512 + w]

        bkcons = {}

        def bk_wait(i):
            return bkcons.get(i)

        def bk_set(i, ev):
            bkcons[i] = ev

        class O:
            pass

        AT = O()
        off = [0]

        def asl(n):
            o = off[0]
            off[0] += n
            return o

        AT.xnT_b = asl(c.DC * c.T)
        AT.wslot = asl(4 * c.D)
        AT.QT2 = asl(c.T)
        AT.KT2 = asl(c.T)
        AT.V2 = asl(c.T)
        AT.Pbuf = asl(c.NV * c.T)
        AT.PTb = asl(4 * c.SW)
        AT.attst = asl(2 * c.SW)
        AT.lsum = asl(4 * (c.NG + 1) + c.NV)
        AT.x2 = asl(2 * c.D)
        AT.xn2 = asl(2 * c.D)
        AT.xnT1 = asl(c.D)
        AW = off[0]
        if 2 * c.TB + 2 * c.D <= c.NV * c.T and c.TBT <= 4:
            AT.att_fc = AT.Pbuf
            AT.wo2 = AT.Pbuf + 2 * c.TB
            AT.xtile = AT.x2
            AT.ost = AT.xnT1
        else:
            AT.att_fc = asl(2 * c.TB)
            AT.wo2 = asl(2 * c.D)
            AT.xtile = asl(c.TBT * c.D)
            AT.ost = asl(2 * c.DW)
            AW = off[0]
        AT.zb = AT.Pbuf
        off[0] = 0
        MO = O()
        CAPT = c.CAP // 128
        MO.idraw = asl(128)
        MO.idcol = asl(2 * CAPT)
        MO.idu = asl(2 * CAPT)
        MO.xbst = asl(6 * c.D)
        MO.xbT = asl(2 * c.DC * c.BS)
        MO.hmidT = asl(c.FFC * c.BS)
        MO.w1 = asl(4 * c.D)
        MO.w2 = asl(4 * c.D)
        MO.ybst = asl(2 * c.D)
        MO.b2B = asl(c.D)
        MW = off[0]
        MO.g0 = MO.xbst
        MO.g1 = MO.xbst + 4 * c.D
        MO.arb = MO.xbst + 8 * c.D
        MO.outst = MO.xbst + 12 * c.D
        assert 16 * c.D <= 6 * c.D + 2 * c.DC * c.BS + c.FFC * c.BS
        AW = max(AW, MW)
        arena = sb("arena", [128, AW])

        def A(o, n, dtype=None):
            ap = arena[:, o:o + n]
            if dtype is not None:
                ap = ap.bitcast(dtype)
            return ap

        for dst, src in (
            (cs_mask[:].rearrange("p (v w) -> p v w", v=c.NV),
             maskdiag.transpose([1, 0, 2])),
            (cs_ident[:], ident128[:, :]), (cs_ones[:], ones128[:, :]),
            (cs_onesc[:], ones_col[:, :]), (cs_ltri[:], ltri[:, :]),
            (cs_iota8[:], iota8[:, :]), (cs_pref[:], prefmask[:, :]),
            (cs_idxb[:], idxbase[:, :]), (cs_tok0[:], tokid0[:, :]),
            (cs_drop[:], dropidx[:, :]),
            (cs_cos[:], ropeCos2[:, :]), (cs_sin[:], ropeSin2[:, :]),
            (cs_R[:], ropeR[:, :]),
            (cs_bq[:], bqP[:, :]), (cs_bk[:], bkP[:, :]),
            (cs_bvB[:], bvB[:, :]), (cs_boB[:], boB[:, :]),
            (cs_gw[:], gate_wT[:, :]), (cs_gb[:], gate_bR[:, :]),
            (cs_b1[:], b1P[:, :]),
        ):
            p.ev("sync", lambda e, d=dst, s=src: e.dma_start(out=d, in_=s),
                 sem=sCst, dma=True)
        zw = (2 * c.E * c.CAP + 128) // 128
        assert zw <= c.NV * c.T
        zev = p.ev("vector", lambda e: e.memset(A(AT.zb, zw), 0.0), sem=sV)
        idz = p.ev("sync", lambda e: e.dma_start(
            out=idtab_loc.rearrange("(p f) -> p f", p=128),
            in_=A(AT.zb, zw)), waits=[zev], sem=sIZ, dma=True)
        consts_ev = sCst.ev()
        p.do("vector", lambda e: e.memset(cumprev[:], 0.0))

        ldevs = {}
        xn_store_ev = {}
        xnt_tc = {}
        xnt_store_last = {}
        txe_last_ev = {}
        xne_evs = {}
        xnt_ready = {}
        for tt in range(min(2, c.NT)):
            ldevs[tt] = p.ev("sync", lambda e, tt=tt: e.dma_start(
                out=A(AT.x2 + (tt % 2) * c.D, c.D),
                in_=x_j[tt * 128:(tt + 1) * 128, :]), waits=[consts_ev],
                sem=sXL[tt % 2], dma=True)

        sex_prev_box = [None]
        ph1_fin = {}

        def emit_ph1_tile(tt):
            bi = tt % 2
            xt = A(AT.x2 + bi * c.D, c.D)
            xnt = A(AT.xn2 + bi * c.D, c.D)
            p.do("vector", lambda e, xt=xt: e.reduce_sum(
                statb[:, 0:1], xt, axis=AX.X), waits=[ldevs[tt]], drain=True)
            p.do("vector", lambda e: e.tensor_scalar_mul(
                statb[:, 0:1], statb[:, 0:1], 1.0 / c.D), drain=True)
            xg = []
            if tt >= 2:
                xg = [xn_store_ev[tt - 2], txe_last_ev[tt - 2]]
            sqev = p.ev("scalar", lambda e, xt=xt, xnt=xnt: e.activation(
                out=xnt, in_=xt, func=AF.Square,
                accum_out=statb[:, 1:2]),
                waits=[ldevs[tt]] + xg, sem=sS, drain=True)
            p.do("vector", lambda e: e.tensor_scalar(
                statb[:, 2:3], statb[:, 1:2], 1.0 / c.D, float(c.eps),
                ALU.mult, ALU.add), waits=[sqev], drain=True)
            p.do("vector", lambda e: e.tensor_tensor(
                statb[:, 3:4], statb[:, 0:1], statb[:, 0:1], ALU.mult),
                drain=True)
            vev = p.ev("vector", lambda e: e.tensor_tensor(
                statb[:, 2:3], statb[:, 2:3], statb[:, 3:4], ALU.subtract),
                sem=sV, drain=True)
            sqr = p.ev("scalar", lambda e: e.activation(
                out=statb[:, 4:5], in_=statb[:, 2:3], func=AF.Sqrt),
                waits=[vev], sem=sS, drain=True)
            p.do("vector", lambda e: e.reciprocal(
                statb[:, 4:5], statb[:, 4:5]), waits=[sqr], drain=True)
            p.do("vector", lambda e: e.tensor_tensor(
                statb[:, 5:6], statb[:, 0:1], statb[:, 4:5], ALU.mult),
                drain=True)
            vst = p.ev("vector", lambda e: e.tensor_scalar_mul(
                statb[:, 5:6], statb[:, 5:6], -1.0), sem=sV, drain=True)
            xne = p.ev("scalar", lambda e, xt=xt, xnt=xnt: e.activation(
                out=xnt, in_=xt, func=AF.Identity, bias=statb[:, 5:6],
                scale=statb[:, 4:5]), waits=[vst], sem=sS, drain=True)
            xne_evs[tt] = xne
            if tt + 2 < c.NT:
                ldevs[tt + 2] = p.ev("sync", lambda e, tt=tt: e.dma_start(
                    out=A(AT.x2 + (tt % 2) * c.D, c.D),
                    in_=x_j[(tt + 2) * 128:(tt + 3) * 128, :]), waits=[xne],
                    sem=sXL[tt % 2], dma=True)
            xn_store_ev[tt] = p.ev("sync", lambda e, xnt=xnt, tt=tt:
                                   e.dma_start(
                                       out=xn_dram[tt * 128:(tt + 1) * 128,
                                                   :], in_=xnt),
                                   waits=[xne], sem=sXN[tt % 2], dma=True)
            if c.debug:
                xn_store_ev[tt] = p.ev("sync", lambda e, xnt=xnt, tt=tt:
                                       e.dma_start(
                                           out=dbg["xn"][tt * 128:
                                                         (tt + 1) * 128, :],
                                           in_=xnt),
                                       waits=[xne], sem=sXN[tt % 2],
                                       dma=True)
            tevs = []
            for g in range(_cdiv(c.DC, 4)):
                n = min(4, c.DC - g * 4)
                tw = [xne] if g == 0 else []
                if g == 0 and tt > 0:
                    tw.append(xnt_store_last[tt - 1])
                if g == 0:
                    tw.append(bk_wait(7))
                if tevs:
                    tw.append(tevs[-1])
                te = None
                for i in range(n):
                    dc = g * 4 + i
                    if i < n - 1:
                        p.do("tensor", tpose(
                            bank(7)[:, i * 128:(i + 1) * 128],
                            xnt[:, dc * 128:(dc + 1) * 128], cs_ident[:],
                            i == 0, False), waits=tw if i == 0 else ())
                    else:
                        te = p.ev("tensor", tpose(
                            bank(7)[:, i * 128:(i + 1) * 128],
                            xnt[:, dc * 128:(dc + 1) * 128], cs_ident[:],
                            i == 0, True), waits=tw if i == 0 else (),
                            sem=sT)
                tevs.append(p.ev("vector", lambda e, g=g, n=n: e.tensor_copy(
                    A(AT.xnT1 + g * 512, n * 128), bank(7)[:, :n * 128]),
                    waits=[te], sem=sV, drain=True))
            bk_set(7, tevs[-1])
            txe_last_ev[tt] = tevs[-1]
            for dc in range(c.DC):
                ev = p.ev("sync", lambda e, dc=dc, tt=tt: e.dma_start(
                    out=xnT_dram[dc * 128:(dc + 1) * 128,
                                 tt * 128:(tt + 1) * 128],
                    in_=A(AT.xnT1 + dc * 128, 128)), waits=[tevs[-1]],
                    sem=sXNT[tt // c.TKC], dma=True)
                if c.debug:
                    xnt_store_last[tt] = p.ev(
                        "sync", lambda e, dc=dc, tt=tt: e.dma_start(
                            out=dbg["xnT"][dc * 128:(dc + 1) * 128,
                                           tt * 128:(tt + 1) * 128],
                            in_=A(AT.xnT1 + dc * 128, 128)),
                        waits=[tevs[-1]], sem=sDBG, dma=True)
                else:
                    xnt_store_last[tt] = ev
            tpc = c.SW // 128
            if (tt + 1) % tpc == 0:
                bb = tt // c.TKC
                tcc = ((tt % c.TKC) + 1) // tpc - 1
                xnt_tc[(bb, tcc)] = sXNT[bb].ev()

            for dc in range(c.DC):
                gw0 = [tevs[dc // 4]] if dc % 4 == 0 else []
                if dc == 0:
                    gw0.append(bk_wait(4))
                p.do("tensor", mm(
                    bank(4, c.E), A(AT.xnT1 + dc * 128, 128),
                    cs_gw[:, dc * c.E:(dc + 1) * c.E], dc == 0, False),
                    waits=gw0)
            gmm = p.ev("tensor", mm(bank(4, c.E), cs_ones[0:1, :],
                                    cs_gb[0:1, :], False, True), sem=sT)
            sex_prev = sex_prev_box[0]
            p.do("vector", lambda e: e.reduce_max(
                gstage[:, 0:1], bank(4, c.E), axis=AX.X),
                waits=[gmm] + ([sex_prev] if sex_prev else []))
            vex = p.ev("vector", lambda e: e.tensor_scalar_mul(
                gstage[:, 0:1], gstage[:, 0:1], -1.0), sem=sV, drain=True)
            sex = p.ev("scalar", lambda e: e.activation(
                out=gstage[:, 8:8 + c.E], in_=bank(4, c.E), func=AF.Exp,
                bias=gstage[:, 0:1], accum_out=gstage[:, 1:2]),
                waits=[vex], sem=sS)
            sex_prev_box[0] = sex
            bk_set(4, sex)
            p.do("vector", lambda e: e.reciprocal(
                gstage[:, 1:2], gstage[:, 1:2]), waits=[sex], drain=True)
            p.do("vector", lambda e: e.tensor_scalar(
                gstage[:, 8:8 + c.E], gstage[:, 8:8 + c.E], gstage[:, 1:2],
                None, ALU.mult), drain=True)
            p.do("vector", lambda e: e.max(
                gstage[:, 16:24], gstage[:, 8:8 + c.E]), drain=True)
            p.do("vector", lambda e: e.max_index(
                gstage[:, 24:32].bitcast(U32), gstage[:, 16:24],
                gstage[:, 8:8 + c.E]), drain=True)
            p.do("vector", lambda e: e.tensor_copy(
                gstage[:, 26:28], gstage[:, 24:26].bitcast(U32)), drain=True)
            for ch in range(2):
                col = 2 * tt + ch
                p.do("vector", lambda e, ch=ch, col=col: e.tensor_copy(
                    st_p[:, col:col + 1], gstage[:, 16 + ch:17 + ch]),
                    drain=(ch == 0))
                p.do("vector", lambda e, ch=ch, col=col: e.tensor_copy(
                    st_e[:, col:col + 1], gstage[:, 26 + ch:27 + ch]))
                p.do("vector", lambda e, ch=ch, col=col: e.tensor_scalar(
                    st_oh[:, col * c.E:(col + 1) * c.E],
                    cs_iota8[:], gstage[:, 26 + ch:27 + ch], None,
                    ALU.is_equal))
            ohev = p.ev("vector", lambda e: e.drain(), sem=sV)
            ohsl = st_oh[:, 2 * tt * c.E:(2 * tt + 2) * c.E]
            p.do("tensor", mm(bank(5, 2 * c.E), cs_ltri[:], ohsl,
                              True, False), waits=[ohev, bk_wait(5)])
            pmm = p.ev("tensor", mm(bank(5, 2 * c.E), cs_ones[0:1, :],
                                    cumprev[:], False, True), sem=sT)
            cpe = p.ev("tensor", mm(bank(6)[0:1, 0:2 * c.E], cs_onesc[:],
                                    ohsl, True, True),
                       waits=[bk_wait(6)], sem=sT)
            cpv = p.ev("vector", lambda e: e.tensor_tensor(
                cumprev[:], cumprev[:], bank(6)[0:1, 0:2 * c.E], ALU.add),
                waits=[cpe], sem=sV, drain=True)
            bk_set(6, cpv)
            pos_last = None
            for ch in range(2):
                col = 2 * tt + ch
                p.do("vector", lambda e, col=col, ch=ch: e.tensor_tensor(
                    gstage[:, 32:32 + c.E],
                    st_oh[:, col * c.E:(col + 1) * c.E],
                    bank(5, 2 * c.E)[:, ch * c.E:(ch + 1) * c.E], ALU.mult),
                    waits=[pmm], drain=True)
                pos_last = p.ev("vector", lambda e, col=col: e.reduce_sum(
                    st_pos[:, col:col + 1], gstage[:, 32:32 + c.E],
                    axis=AX.X), sem=sV, drain=True)
            bk_set(5, pos_last)
            if tt == c.NT - 1:
                emit_ph1_fin()

        def emit_ph1_fin():
            ph1_fin["v_end"] = p.ev("vector", lambda e: e.drain(), sem=sV)
            ph1_fin["s_end"] = (sS, sS.n)
            ph1_fin["sdone"] = [s.ev() for s in sXN]
            ph1_fin["tdone"] = [s.ev() for s in sXNT]
            cpy = p.ev("sync", lambda e: e.dma_start(out=cnt_in[:, :],
                                                     in_=cumprev[:]),
                       waits=[ph1_fin["v_end"]], sem=sC, dma=True)
            ph1_fin["ag_cnt"] = p.ev("gpsimd", lambda e: e.collective_compute(
                "AllGather", ALU.bypass,
                replica_groups=[list(range(c.n_cores))],
                ins=[cnt_in[:, :]], outs=[cnt_out[:, :]]),
                waits=[cpy], sem=sAGC)
            ph1_fin["ag_xn"] = p.ev("gpsimd", lambda e: e.collective_compute(
                "AllGather", ALU.bypass,
                replica_groups=[list(range(c.n_cores))],
                ins=[xn_dram[:, :]], outs=[xn_full[:, :]]),
                waits=ph1_fin["sdone"], sem=sAGX)
            ph1_fin["acl"] = p.ev("sync", lambda e: e.dma_start(
                out=allcnt[:], in_=cnt_out[:, :]),
                waits=[ph1_fin["ag_cnt"]], sem=sC, dma=True)
            if c.debug:
                p.ev("sync", lambda e: e.dma_start(
                    out=dbg["cnt"][:, :], in_=allcnt[:]),
                    waits=[ph1_fin["acl"]], sem=sDBG, dma=True)

        ph1_head = c.TKC if c.BL > 1 else c.NT
        for tt in range(ph1_head):
            emit_ph1_tile(tt)
        ph1_next = [ph1_head]

        def emit_ph1_chunk(k):
            tgt = min(c.NT, ph1_next[0] + k)
            while ph1_next[0] < tgt:
                emit_ph1_tile(ph1_next[0])
                ph1_next[0] += 1

        if stop_after == "ph1":
            while ph1_next[0] < c.NT:
                emit_ph1_tile(ph1_next[0])
                ph1_next[0] += 1
            _finish(p, all_sems)
            _emit(nc, p)
            return nc

        def emit_routing():
            tof = p.ev("tensor", mm(bank(4)[0:1, 0:2 * c.E], cs_pref[:],
                                    allcnt[:], True, True),
                       waits=[ph1_fin["acl"], bk_wait(4), (sV, sV.n)],
                       sem=sT)
            cpo = p.ev("vector", lambda e: e.tensor_copy(
                gstage[0:1, 32:32 + 2 * c.E], bank(4)[0:1, 0:2 * c.E]),
                waits=[tof], sem=sV)
            bro = p.ev("tensor", mm(bank(4, 2 * c.E), cs_ones[0:1, :],
                                    gstage[0:1, 32:32 + 2 * c.E],
                                    True, True), waits=[cpo], sem=sT)
            offs_ev = p.ev("vector", lambda e: e.tensor_copy(
                offsb[:], bank(4, 2 * c.E)), waits=[bro], sem=sV)
            bk_set(4, offs_ev)
            BSC = 512
            g = gstage
            sct_evs = []
            for tt in range(c.NT):
                for ch in range(2):
                    col = 2 * tt + ch
                    ohsl = st_oh[:, col * c.E:(col + 1) * c.E]
                    p.do("gpsimd", lambda e, ohsl=ohsl, ch=ch: e.tensor_tensor(
                        g[:, 32:32 + c.E], ohsl,
                        offsb[:, ch * c.E:(ch + 1) * c.E], ALU.mult),
                        waits=[offs_ev]
                        if tt == 0 and ch == 0 else (), drain=True)
                    p.do("gpsimd", lambda e: e.tensor_tensor(
                        g[:, 32:36], g[:, 32:36], g[:, 36:40], ALU.add),
                        drain=True)
                    p.do("gpsimd", lambda e: e.tensor_tensor(
                        g[:, 32:34], g[:, 32:34], g[:, 34:36], ALU.add),
                        drain=True)
                    p.do("gpsimd", lambda e: e.tensor_tensor(
                        g[:, 40:41], g[:, 32:33], g[:, 33:34], ALU.add),
                        drain=True)
                    p.do("gpsimd", lambda e, col=col: e.tensor_tensor(
                        st_pos[:, col:col + 1], st_pos[:, col:col + 1],
                        g[:, 40:41], ALU.add), drain=True)
                    p.do("gpsimd", lambda e, col=col: e.tensor_scalar(
                        g[:, 41:42], st_pos[:, col:col + 1],
                        float(c.CAP), None, ALU.is_lt), drain=True)
                    p.do("gpsimd", lambda e, col=col: e.tensor_tensor(
                        st_w[:, col:col + 1], st_p[:, col:col + 1],
                        g[:, 41:42], ALU.mult), drain=True)
                    p.do("gpsimd", lambda e, col=col, ch=ch: e.tensor_scalar(
                        g[:, 42:43], st_e[:, col:col + 1], float(c.CAP),
                        float(ch * c.E * c.CAP), ALU.mult, ALU.add))
                    p.do("gpsimd", lambda e, col=col: e.tensor_tensor(
                        g[:, 42:43], g[:, 42:43], st_pos[:, col:col + 1],
                        ALU.add), drain=True)
                    p.do("gpsimd", lambda e: e.tensor_scalar(
                        g[:, 42:43], g[:, 42:43], g[:, 41:42],
                        None, ALU.mult), drain=True)
                    p.do("gpsimd", lambda e: e.tensor_scalar(
                        g[:, 43:44], g[:, 41:42], -1.0, 1.0,
                        ALU.mult, ALU.add), drain=True)
                    p.do("gpsimd", lambda e: e.tensor_tensor(
                        g[:, 43:44], g[:, 43:44], cs_drop[:], ALU.mult),
                        drain=True)
                    p.do("gpsimd", lambda e, col=col: e.tensor_tensor(
                        g[:, 48:49], g[:, 42:43],
                        g[:, 43:44], ALU.add), drain=True)
                    p.do("gpsimd", lambda e, col=col, ch=ch: e.tensor_scalar(
                        g[:, 44:45], st_pos[:, col:col + 1], 1.0,
                        float(ch * c.CAP), ALU.mult, ALU.add))
                    p.do("gpsimd", lambda e: e.tensor_scalar_min(
                        g[:, 44:45], g[:, 44:45], float(c.NSLOT - 1)),
                        drain=True)
                    p.do("gpsimd", lambda e: e.tensor_copy(
                        g[:, 45:46].bitcast(U32), g[:, 44:45]), drain=True)
                    p.do("gpsimd", lambda e: e.tensor_scalar(
                        g[:, 52:53], g[:, 44:45], 1.0 / BSC,
                        -255.5 / BSC, ALU.mult, ALU.add), drain=True)
                    p.do("gpsimd", lambda e: e.tensor_copy(
                        g[:, 46:47].bitcast(U32), g[:, 52:53]), drain=True)
                    p.do("gpsimd", lambda e: e.tensor_scalar(
                        g[:, 53:54].bitcast(U32), g[:, 46:47].bitcast(U32),
                        BSC, None, ALU.mult), drain=True)
                    p.do("gpsimd", lambda e: e.tensor_tensor(
                        g[:, 47:48].bitcast(U32), g[:, 45:46].bitcast(U32),
                        g[:, 53:54].bitcast(U32), ALU.subtract), drain=True)
                    p.do("gpsimd", lambda e: e.tensor_scalar(
                        g[:, 46:47].bitcast(U32), g[:, 46:47].bitcast(U32),
                        c.n_cores * BSC, None, ALU.mult), drain=True)
                    p.do("gpsimd", lambda e, col=col: e.tensor_copy(
                        g[:, 49:50].bitcast(U32), st_e[:, col:col + 1]),
                        drain=True)
                    p.do("gpsimd", lambda e: e.tensor_scalar(
                        g[:, 49:50].bitcast(U32), g[:, 49:50].bitcast(U32),
                        BSC, None, ALU.mult), drain=True)
                    p.do("gpsimd", lambda e: e.tensor_tensor(
                        g[:, 46:47].bitcast(U32), g[:, 46:47].bitcast(U32),
                        g[:, 49:50].bitcast(U32), ALU.add), drain=True)
                    p.do("gpsimd", lambda e, col=col: e.tensor_tensor(
                        st_row[:, col:col + 1], g[:, 46:47].bitcast(U32),
                        g[:, 47:48].bitcast(U32), ALU.add), drain=True)
                    sc = 50 if ch == 0 else 54
                    p.do("gpsimd", lambda e, tt=tt, sc=sc:
                         e.tensor_scalar_add(
                             g[:, sc:sc + 1], cs_tok0[:], float(128 * tt)),
                         waits=[sct_evs[-2]] if len(sct_evs) >= 2 else (),
                         drain=True)
                    p.do("gpsimd", lambda e, col=col, sc=sc: e.tensor_copy(
                        g[:, sc + 1:sc + 2].bitcast(U32), g[:, 48:49]),
                        drain=True)
                    sct_evs.append(p.ev(
                        "gpsimd", lambda e, sc=sc: e.indirect_dma_start(
                            out=idtab_loc.rearrange("(a b) -> a b", b=1),
                            out_offset=IndirectOffsetOnAxis(
                                g[:, sc + 1:sc + 2].bitcast(U32), 0),
                            in_=g[:, sc:sc + 1], in_offset=None),
                        waits=([idz] if tt == 0 and ch == 0
                               else [sct_evs[-1]]),
                        sem=sSCT, dma=True, drain=True))
            scat_done = sSCT.ev()
            rt_done = p.ev("gpsimd", lambda e: e.drain(), sem=sG)
            rt_done_box[0] = rt_done
            ar_id = p.ev("gpsimd", lambda e: e.collective_compute(
                "AllReduce", ALU.add,
                replica_groups=[list(range(c.n_cores))],
                ins=[idtab_loc[:]], outs=[idtab_full[:]]),
                waits=[scat_done], sem=sARI)
            if c.debug:
                p.ev("sync", lambda e: e.dma_start(
                    out=dbg["idtab"][:, :],
                    in_=idtab_full[0:2 * c.E * c.CAP].rearrange(
                        "(a b) -> a b", a=2)), waits=[ar_id], sem=sDBG,
                    dma=True)
            return ar_id

        rt_done_box = [None]

        if stop_after == "route":
            emit_routing()
            _finish(p, all_sems)
            _emit(nc, p)
            return nc

        xnT_b = A(AT.xnT_b, c.DC * c.T)
        QT2 = A(AT.QT2, c.T)
        KT2 = A(AT.KT2, c.T)
        V2 = A(AT.V2, c.T)
        Pbuf = A(AT.Pbuf, c.NV * c.T)
        lsum = A(AT.lsum, 4 * (c.NG + 1) + c.NV)
        SW = c.SW
        LS_R = 4 * c.NG

        wsrc = []
        for b in range(c.BL):
            for hp in range(c.HPAIR):
                wsrc += [(WqH, hp), (WkH, hp), (WvH, hp)]
        wload_evs = {}
        wcons_evs = {}

        def wload(idx, waits=()):
            W, hp = wsrc[idx]
            slot = idx % 4
            wload_evs[idx] = p.ev(
                "scalar", lambda e, W=W, hp=hp, slot=slot: e.dma_start(
                    out=A(AT.wslot + slot * c.D, c.D), in_=W[hp]),
                waits=list(waits), sem=sWS[idx % 4], dma=True)

        for idx in range(min(4, len(wsrc))):
            wload(idx, waits=[consts_ev] if idx == 0 else ())

        def wbuf(idx):
            return A(AT.wslot + (idx % 4) * c.D, c.D)

        xb_tc = {}
        xb_evs = {}

        def xnT_load(b, extra=()):
            q = "sync" if b == 0 else "gpsimd"
            for tc in range(c.T // SW):
                for dc in range(c.DC):
                    xb_tc[(b, tc)] = p.ev(
                        q, lambda e, dc=dc, b=b, tc=tc: e.dma_start(
                            out=xnT_b[:, dc * c.T + tc * SW:
                                      dc * c.T + (tc + 1) * SW],
                            in_=xnT_dram[dc * 128:(dc + 1) * 128,
                                         b * c.T + tc * SW:
                                         b * c.T + (tc + 1) * SW]),
                        waits=[xnt_tc[(b, tc)]] + list(extra)
                        if dc == 0 else (),
                        sem=sXB[tc % 4], dma=True)
            return xb_tc[(b, c.T // SW - 1)]

        routing_done = [None]
        wo_free = [None, None]
        af_free = [None, None]
        gctr = [0]
        attst_evac_evs = {}
        attst_store_evs = {}
        o_stores = {}
        af_done = {}
        avt_prev_last = None
        sc_rot = [0]
        last_xnTb_reader = [None]
        for b in range(c.BL):
            xb_evs[b] = xnT_load(
                b, extra=[last_xnTb_reader[0]] if b > 0 else ())
            xb_ev = xb_evs[b]
            for hp in range(c.HPAIR):
                sidx = b * c.HPAIR + hp
                widx = 3 * sidx
                wq, wk, wv = wbuf(widx), wbuf(widx + 1), wbuf(widx + 2)
                qk_evac = []
                rope_v = []
                for pi, (wsl, dstb, biasb) in enumerate(
                        ((wq, QT2, cs_bq), (wk, KT2, cs_bk))):
                    for tc in range(c.T // SW):
                        bk = tc % 2
                        gi = pi * (c.T // SW) + tc
                        w0 = [bk_wait(bk), xb_tc[(b, tc)]]
                        if gi == 0:
                            w0.append(wload_evs[widx + pi])
                        if tc == 0 and pi == 1:
                            w0.append(wload_evs[widx + 1])
                        lastmm = None
                        for dc in range(c.DC):
                            if dc < c.DC - 1:
                                p.do("tensor", mm(
                                    bank(bk, SW),
                                    wsl[:, dc * 128:(dc + 1) * 128],
                                    xnT_b[:, dc * c.T + tc * SW:
                                          dc * c.T + (tc + 1) * SW],
                                    dc == 0, False),
                                    waits=w0 if dc == 0 else ())
                            else:
                                lastmm = p.ev("tensor", mm(
                                    bank(bk, SW),
                                    wsl[:, dc * 128:(dc + 1) * 128],
                                    xnT_b[:, dc * c.T + tc * SW:
                                          dc * c.T + (tc + 1) * SW],
                                    False, True),
                                    waits=w0 if c.DC == 1 else (), sem=sT)
                        qe = p.ev(
                            "scalar",
                            lambda e, dstb=dstb, biasb=biasb, hp=hp, tc=tc,
                            bk=bk: e.activation(
                                out=dstb[:, tc * SW:(tc + 1) * SW],
                                in_=bank(bk, SW), func=AF.Identity,
                                bias=biasb[:, hp:hp + 1]),
                            waits=[lastmm], sem=sS)
                        qk_evac.append(qe)
                        bk_set(bk, qe)
                    for tc in range(c.T // SW):
                        bk = 2 + tc % 2
                        rw = [qk_evac[pi * (c.T // SW) + tc], bk_wait(bk)]
                        rte = p.ev("tensor", mm(
                            bank(bk, SW), cs_R[:],
                            dstb[:, tc * SW:(tc + 1) * SW], True, True),
                            waits=rw, sem=sT)
                        p.do("vector",
                             lambda e, dstb=dstb, tc=tc: e.tensor_tensor(
                                 dstb[:, tc * SW:(tc + 1) * SW],
                                 dstb[:, tc * SW:(tc + 1) * SW],
                                 cs_cos[:, tc * SW:(tc + 1) * SW], ALU.mult),
                             waits=[rte])
                        p.do("vector", lambda e, bk=bk, tc=tc:
                             e.tensor_tensor(
                                 bank(bk, SW), bank(bk, SW),
                                 cs_sin[:, tc * SW:(tc + 1) * SW], ALU.mult))
                        rv = p.ev(
                            "vector",
                            lambda e, dstb=dstb, tc=tc, bk=bk:
                            e.tensor_tensor(
                                dstb[:, tc * SW:(tc + 1) * SW],
                                dstb[:, tc * SW:(tc + 1) * SW],
                                bank(bk, SW), ALU.add), sem=sV, drain=True)
                        rope_v.append(rv)
                        bk_set(bk, rv)
                rope_all = rope_v[-1]
                qk_done = qk_evac[-1]
                v_evac = []
                v_mm = []
                for kt2 in range(c.TKC // 2):
                    bk = kt2 % 2
                    w0 = [bk_wait(bk)]
                    if kt2 >= 2:
                        w0.append(v_mm[kt2 - 1])
                    elif kt2 == 0:
                        w0 += [wload_evs[widx + 2], xb_ev]
                    lastmm = None
                    for sub in range(2):
                        kt = kt2 * 2 + sub
                        for dc in range(c.DC):
                            last = (sub == 1 and dc == c.DC - 1)
                            if not last:
                                p.do("tensor", mm(
                                    bank(bk)[:, sub * 128:(sub + 1) * 128],
                                    xnT_b[:, dc * c.T + kt * 128:
                                          dc * c.T + (kt + 1) * 128],
                                    wv[:, dc * 128:(dc + 1) * 128],
                                    dc == 0, dc == c.DC - 1),
                                    waits=w0 if sub == 0 and dc == 0 else ())
                            else:
                                lastmm = p.ev("tensor", mm(
                                    bank(bk)[:, sub * 128:(sub + 1) * 128],
                                    xnT_b[:, dc * c.T + kt * 128:
                                          dc * c.T + (kt + 1) * 128],
                                    wv[:, dc * 128:(dc + 1) * 128],
                                    dc == 0, True), sem=sT)
                    v_mm.append(lastmm)
                    for sub in range(2):
                        kt = kt2 * 2 + sub
                        ve = p.ev(
                            "vector", lambda e, kt=kt, bk=bk, sub=sub, hp=hp:
                            e.tensor_tensor(
                                V2[:, kt * 128:(kt + 1) * 128],
                                bank(bk)[:, sub * 128:(sub + 1) * 128],
                                cs_bvB[:, hp * 128:(hp + 1) * 128],
                                ALU.add), waits=[lastmm], sem=sV)
                        v_evac.append(ve)
                    bk_set(bk, ve)
                v_done = v_evac[-1]
                last_xnTb_reader[0] = v_mm[-1]
                if widx + 4 < len(wsrc):
                    wload(widx + 4, waits=[qk_evac[c.T // SW - 1]])
                if widx + 5 < len(wsrc):
                    wload(widx + 5, waits=[qk_done])
                if widx + 6 < len(wsrc):
                    wload(widx + 6, waits=[v_mm[-1]])
                first_box = [True]
                sched = []
                for h2x in range(2):
                    for gx in range(c.NG):
                        for qix in range(c.NV):
                            for kcx in range(gx + 1):
                                sched.append((h2x, gx, qix, kcx))
                exp_by_idx = {}
                pend = []
                mm_done = [0]

                def emit_score_mm(idx):
                    h2x, gx, qix, kcx = sched[idx]
                    hbx = h2x * 64
                    nchx = gx + 1
                    qtx = gx * c.NV + qix
                    bkx = sc_rot[0] % 4
                    sc_rot[0] += 1
                    w0 = [bk_wait(bkx)]
                    ew_extra = []
                    if first_box[0]:
                        w0.append(rope_all)
                        if b == 0 and sidx == 0:
                            ew_extra.append(idz)
                        if b > 0 and sidx == b * c.HPAIR:
                            ew_extra += list(af_done[b - 1])
                        first_box[0] = False
                    smm = p.ev("tensor", mm(
                        bank(bkx, SW),
                        QT2[hbx:hbx + 64, qtx * 128:(qtx + 1) * 128],
                        KT2[hbx:hbx + 64, kcx * SW:(kcx + 1) * SW],
                        True, True), waits=w0, sem=sT)
                    if kcx == nchx - 1:
                        gate = p.ev(
                            "vector", lambda e, bkx=bkx, qix=qix:
                            e.tensor_tensor(
                                bank(bkx, SW), bank(bkx, SW),
                                cs_mask[:, qix * SW:(qix + 1) * SW],
                                ALU.add), waits=[smm], sem=sV)
                    else:
                        gate = smm
                    mm_done[0] += 1
                    return (idx, qix, kcx, gate, bkx, ew_extra)

                def emit_score_exp(rec, extra=()):
                    idx, qix, kcx, gate, bkx, ew = rec
                    ev = p.ev(
                        "scalar",
                        lambda e, qix=qix, kcx=kcx, bkx=bkx: e.activation(
                            out=Pbuf[:, qix * c.T + kcx * SW:
                                     qix * c.T + (kcx + 1) * SW],
                            in_=bank(bkx, SW), func=AF.Exp, scale=0.125,
                            accum_out=lsum[:, qix * c.NG + kcx:
                                           qix * c.NG + kcx + 1]),
                        waits=[gate] + ew + list(extra), sem=sS)
                    bk_set(bkx, ev)
                    exp_by_idx[idx] = ev

                ci = 0
                for h2 in range(2):
                    hb = h2 * 64
                    pt_evac = {}
                    avt_mm = {}
                    for g in range(c.NG):
                        nch = g + 1
                        dgevs = []
                        dg2evs = []
                        for qi in range(c.NV):
                            for kc in range(nch):
                                if ci >= mm_done[0]:
                                    rec = emit_score_mm(ci)
                                    emit_score_exp(rec)
                                ci += 1
                            p.do("vector", lambda e, qi=qi, nch=nch:
                                 e.reduce_sum(
                                     lsum[:, LS_R + qi:LS_R + qi + 1],
                                     lsum[:, qi * c.NG:qi * c.NG + nch],
                                     axis=AX.X),
                                 waits=[exp_by_idx[ci - 1]])
                            rle = p.ev("vector", lambda e, qi=qi:
                                       e.reciprocal(
                                           lsum[:, LS_R + qi:LS_R + qi + 1],
                                           lsum[:, LS_R + qi:LS_R + qi + 1]),
                                       sem=sV, drain=True)
                            hd1 = min(256, nch * SW)
                            dgevs.append(p.ev(
                                "scalar",
                                lambda e, qi=qi, hd1=hd1: e.activation(
                                    out=Pbuf[:, qi * c.T:qi * c.T + hd1],
                                    in_=Pbuf[:, qi * c.T:qi * c.T + hd1],
                                    func=AF.Identity,
                                    scale=lsum[:, LS_R + qi:
                                               LS_R + qi + 1]),
                                waits=[rle], sem=sS, drain=True))
                            if nch * SW > hd1:
                                dg2evs.append(p.ev(
                                    "scalar",
                                    lambda e, qi=qi, nch=nch, hd1=hd1:
                                    e.activation(
                                        out=Pbuf[:, qi * c.T + hd1:
                                                 qi * c.T + nch * SW],
                                        in_=Pbuf[:, qi * c.T + hd1:
                                                 qi * c.T + nch * SW],
                                        func=AF.Identity,
                                        scale=lsum[:, LS_R + qi:
                                                   LS_R + qi + 1]),
                                    sem=sS, drain=True))
                            else:
                                dg2evs.append(dgevs[-1])
                        npre = min(4, len(sched) - mm_done[0])
                        for _ in range(npre):
                            pend.append(emit_score_mm(mm_done[0]))
                        gc = gctr[0]
                        nkc = nch * c.NV
                        av_bank = 6 + gc % 2
                        ptm_last = None
                        for step in range(nkc + 1):
                            kc = step
                            if kc < nkc:
                                bk = 4 + kc % 2
                                w0 = []
                                if kc < 2:
                                    w0.append(bk_wait(bk))
                                if kc == 1:
                                    w0.append(dgevs[-1])
                                if kc == 2:
                                    w0.append(dg2evs[-1])
                                if kc >= 2:
                                    w0.append(pt_evac[kc - 2])
                                ptm = None
                                for qi in range(c.NV):
                                    qw = list(w0) if qi == 0 else []
                                    if kc == 0:
                                        qw.append(dgevs[qi])
                                    if qi < c.NV - 1:
                                        p.do("tensor", tpose(
                                            bank(bk)[:, qi * 128:
                                                     (qi + 1) * 128],
                                            Pbuf[:, qi * c.T + kc * 128:
                                                 qi * c.T + (kc + 1) * 128],
                                            cs_ident[:], qi == 0, False),
                                            waits=qw)
                                    else:
                                        ptm = p.ev("tensor", tpose(
                                            bank(bk)[:, qi * 128:
                                                     (qi + 1) * 128],
                                            Pbuf[:, qi * c.T + kc * 128:
                                                 qi * c.T + (kc + 1) * 128],
                                            cs_ident[:], qi == 0, True),
                                            waits=qw, sem=sT)
                                pw = [ptm]
                                if kc >= 3:
                                    pw.append(avt_mm[kc - 3])
                                elif avt_prev_last is not None:
                                    pw.append(avt_prev_last)
                                pt_evac[kc] = p.ev(
                                    "vector", lambda e, kc=kc, bk=bk:
                                    e.tensor_copy(
                                        A(AT.PTb + (kc % 4) * SW, SW),
                                        bank(bk, SW)), waits=pw, sem=sV)
                                bk_set(bk, pt_evac[kc])
                                ptm_last = ptm
                            ac = step - 1
                            if ac >= 0:
                                aw = [pt_evac[ac]]
                                if ac == 0:
                                    aw.append(v_done)
                                    aw.append(bk_wait(av_bank))
                                last = ac == nkc - 1
                                avt_mm[ac] = p.ev("tensor", mm(
                                    bank(av_bank)[0:64, 0:SW],
                                    V2[:, ac * 128 + hb:ac * 128 + hb + 64],
                                    A(AT.PTb + (ac % 4) * SW, SW),
                                    ac == 0, last), waits=aw, sem=sT)
                        amm = avt_mm[nkc - 1]
                        avt_prev_last = amm
                        for j, rec in enumerate(pend):
                            emit_score_exp(rec,
                                           extra=[ptm_last] if j == 0 else ())
                        pend = []

                        avw = [amm, attst_store_evs[gc - 2]
                               if gc >= 2 else None]
                        attst_evac_evs[gc] = p.ev(
                            "vector", lambda e, gc=gc, av_bank=av_bank:
                            e.tensor_copy(
                                A(AT.attst + (gc % 2) * SW, SW)[0:64, :],
                                bank(av_bank)[0:64, 0:SW]),
                            waits=avw, sem=sV)
                        bk_set(av_bank, attst_evac_evs[gc])
                        attst_store_evs[gc] = p.ev(
                            "sync", lambda e, hp=hp, hb=hb, g=g, b=b, gc=gc:
                            e.dma_start(
                                out=attnT_dram[
                                    b, hp * 128 + hb:hp * 128 + hb + 64,
                                    g * SW:(g + 1) * SW],
                                in_=A(AT.attst + (gc % 2) * SW,
                                      SW)[0:64, :]),
                            waits=[attst_evac_evs[gc]],
                            sem=sAS[gc % 2], dma=True)
                        gctr[0] += 1
                rt_at = c.HPAIR if c.BL > 1 else 0
                if sidx == rt_at and stop_after is None:
                    routing_done[0] = emit_routing()
                nhp_i = min(c.HPAIR, 6)
                if b == 0 and c.BL > 1 and hp < nhp_i:
                    emit_ph1_chunk(_cdiv(c.NT - c.TKC, nhp_i))
            att_done = [sAS[0].ev(), sAS[1].ev()]
            nfc = c.DC
            NGR = (c.T // c.TB) * nfc
            wo_evs = {}
            af_evs = {}
            o_mm_evs = {}

            def wo_load(gidx, waits=()):
                fc = gidx % nfc
                return p.ev("sync", lambda e, fc=fc, gidx=gidx: e.dma_start(
                    out=A(AT.wo2 + (gidx % 2) * c.D, c.D),
                    in_=WoT[fc * 128:(fc + 1) * 128, :]),
                    waits=list(waits), sem=sWO[gidx % 2], dma=True)

            def af_load(gidx, b, waits=()):
                ttb = gidx // nfc
                fc = gidx % nfc
                return p.ev("gpsimd",
                            lambda e, ttb=ttb, fc=fc, gidx=gidx, b=b:
                            e.dma_start(
                                out=A(AT.att_fc + (gidx % 2) * c.TB, c.TB),
                                in_=attnT_dram[b, fc * 128:(fc + 1) * 128,
                                               ttb * c.TB:(ttb + 1) * c.TB]),
                            waits=list(waits), sem=sAF[gidx % 2],
                            dma=True)

            xt_evs = {}

            def xt_load(ttb, tq, b, waits=()):
                tok0 = b * c.T + ttb * c.TB + tq * 128
                return p.ev("gpsimd",
                            lambda e, tok0=tok0, tq=tq: e.dma_start(
                                out=A(AT.xtile + tq * c.D, c.D),
                                in_=x_j[tok0:tok0 + 128, :]),
                            waits=list(waits), sem=sXT, dma=True)

            o_evac_ev = {}
            def st_gc(fc, h2, g):
                return ((b * c.HPAIR + fc) * 2 + h2) * c.NG + g

            for ttb in range(c.T // c.TB):
                if ttb == 0:
                    ph1g = [] if b > 0 else (
                        [ph1_fin["s_end"], txe_last_ev[c.NT - 1]] +
                        ph1_fin["sdone"])
                    wo_evs[0] = wo_load(0, waits=att_done)
                    af_evs[0] = af_load(0, b, waits=att_done)
                    wo_evs[1] = wo_load(1)
                    af_evs[1] = af_load(1, b)
                    for tq in range(c.TBT):
                        xt_evs[tq] = xt_load(
                            0, tq, b,
                            waits=(ph1g if tq == 0 and b == 0 else
                                   (list(o_stores[b - 1])
                                    if tq == 0 and b > 0 else ())))
                    xt_all = sXT.ev()
                for fc in range(nfc):
                    gidx = ttb * nfc + fc
                    w0 = [wo_evs[gidx], af_evs[gidx]]
                    if fc == 0:
                        if ttb > 0:
                            w0.append(o_evac_ev[ttb - 1])
                        else:
                            w0 += [bk_wait(i) for i in range(8)]
                    lastmm = None
                    ntl = c.TBT * c.NDW
                    for tq in range(c.TBT):
                        for dc2 in range(c.NDW):
                            tile = tq * c.NDW + dc2
                            last = (tile == ntl - 1)
                            if not last:
                                p.do("tensor", mm(
                                    bank(tile, c.DW),
                                    A(AT.att_fc + (gidx % 2) * c.TB,
                                      c.TB)[:, tq * 128:(tq + 1) * 128],
                                    A(AT.wo2 + (gidx % 2) * c.D,
                                      c.D)[:, dc2 * c.DW:(dc2 + 1) * c.DW],
                                    fc == 0, False),
                                    waits=w0 if tile == 0 else ())
                            else:
                                lastmm = p.ev("tensor", mm(
                                    bank(tile, c.DW),
                                    A(AT.att_fc + (gidx % 2) * c.TB,
                                      c.TB)[:, tq * 128:(tq + 1) * 128],
                                    A(AT.wo2 + (gidx % 2) * c.D,
                                      c.D)[:, dc2 * c.DW:(dc2 + 1) * c.DW],
                                    fc == 0, fc == nfc - 1), sem=sT)
                    o_mm_evs[gidx] = lastmm
                    if gidx + 2 < NGR:
                        nfcx = (gidx + 2) % nfc
                        nttb = (gidx + 2) // nfc
                        wo_evs[gidx + 2] = wo_load(gidx + 2,
                                                   waits=[lastmm])
                        af_evs[gidx + 2] = af_load(
                            gidx + 2, b,
                            waits=[lastmm,
                                   attst_store_evs[st_gc(nfcx, 0, nttb)],
                                   attst_store_evs[st_gc(nfcx, 1, nttb)]])
                oend = o_mm_evs[ttb * nfc + nfc - 1]
                last_evac = None
                if ttb == 0 and b == 0:
                    ost_prev = {}
                for tq in range(c.TBT):
                    for dc2 in range(c.NDW):
                        tile = tq * c.NDW + dc2
                        ow = [oend, xt_all]
                        if b == 0 and ttb == 0 and tile == 0:
                            ow += ph1_fin["tdone"]
                        p.do("vector", lambda e, tile=tile, dc2=dc2:
                             e.tensor_tensor(
                                 bank(tile, c.DW), bank(tile, c.DW),
                                 cs_boB[:, dc2 * c.DW:(dc2 + 1) * c.DW],
                                 ALU.add), waits=ow)
                        ove = p.ev("vector",
                                   lambda e, tile=tile, tq=tq, dc2=dc2:
                                   e.tensor_tensor(
                                       A(AT.ost + (tile % 2) * c.DW, c.DW),
                                       bank(tile, c.DW),
                                       A(AT.xtile + tq * c.D,
                                         c.D)[:, dc2 * c.DW:
                                              (dc2 + 1) * c.DW],
                                       ALU.add),
                                   waits=[ost_prev.get(tile % 2)],
                                   sem=sV, drain=True)
                        last_evac = ove
                        bk_set(tile, ove)
                        tok0 = b * c.T + ttb * c.TB + tq * 128
                        ost_prev[tile % 2] = p.ev(
                            "sync", lambda e, tile=tile, tok0=tok0,
                            dc2=dc2: e.dma_start(
                                out=attres_dram[
                                    tok0:tok0 + 128,
                                    dc2 * c.DW:(dc2 + 1) * c.DW],
                                in_=A(AT.ost + (tile % 2) * c.DW, c.DW)),
                            waits=[ove], sem=sOS[tile % 2], dma=True)
                o_evac_ev[ttb] = last_evac
                if ttb + 1 < c.T // c.TB:
                    for tq in range(c.TBT):
                        xt_evs[tq] = xt_load(ttb + 1, tq, b,
                                             waits=[last_evac]
                                             if tq == 0 else ())
                    xt_all = sXT.ev()
            o_stores[b] = [sOS[0].ev(), sOS[1].ev()]

            af_done[b] = (sAF[0].ev(), sAF[1].ev(),
                          sWO[0].ev(), sWO[1].ev())
        attn_v_last = sV.ev()
        if c.debug:
            p.ev("sync", lambda e: e.dma_start(
                out=dbg["attres"][:, :], in_=attres_dram[:, :]),
                waits=list(o_stores[c.BL - 1]), sem=sDBG, dma=True)
        if stop_after == "attn":
            routing_done[0] = routing_done[0] or emit_routing()
            _finish(p, all_sems)
            _emit(nc, p)
            return nc

        ar_id = routing_done[0]

        idraw = A(MO.idraw, 128)
        idcol = A(MO.idcol, 2 * CAPT)
        idu = A(MO.idu, 2 * CAPT, U32)
        itev = []
        for ch in range(2):
            gie = p.ev("gpsimd", lambda e, ch=ch: e.indirect_dma_start(
                out=idraw[:], out_offset=None,
                in_=idtab_full.rearrange("(a b) -> a b", b=128),
                in_offset=IndirectOffsetOnAxis(cs_idxb[:, ch:ch + 1], 0)),
                waits=[ar_id] + list(o_stores[c.BL - 1]) if ch == 0
                      else [itev[0][0]],
                sem=sID, dma=True)
            ite = p.ev("tensor", tpose(bank(2)[:, 0:128], idraw[:],
                                       cs_ident[:], True, True),
                       waits=[gie, bk_wait(2)], sem=sT)
            cv = p.ev("vector", lambda e, ch=ch: e.tensor_copy(
                idcol[:, ch * CAPT:(ch + 1) * CAPT], bank(2)[:, 0:CAPT]),
                waits=[ite], sem=sV)
            bk_set(2, cv)
            itev.append((ite, cv))
        idu_ev = p.ev("vector", lambda e: e.tensor_copy(idu[:], idcol[:]),
                      waits=[itev[-1][1]], sem=sV, drain=True)
        b2ld = p.ev("sync", lambda e: e.dma_start(
            out=A(MO.b2B, c.D), in_=b2B[:, :]),
            waits=list(o_stores[c.BL - 1]), sem=sC, dma=True)

        ag_xn_ev = sAGX.ev()
        hmidT = A(MO.hmidT, c.FFC * c.BS)
        xbst = MO.xbst
        gth_evs = {}
        str_evs = {}

        def gather(stile):
            par = stile % 6
            w = [idu_ev, ag_xn_ev] if stile == 0 else []
            if stile >= 6:
                w.append(str_evs[stile - 6])
            gth_evs[stile] = p.ev(
                "gpsimd", lambda e, stile=stile, par=par:
                e.indirect_dma_start(
                    out=A(xbst + par * c.D, c.D),
                    out_offset=None, in_=xn_full[:, :],
                    in_offset=IndirectOffsetOnAxis(
                        idu[:, stile:stile + 1], 0)),
                waits=w, sem=sGA[stile % 6], dma=True)

        nst_tot = c.NSB * c.BST
        for stile in range(min(6, nst_tot)):
            gather(stile)

        w1_evs = {}
        w2_evs = {}

        def w1_load(ffc, waits=()):
            w1_evs[ffc] = p.ev(
                "sync", lambda e, ffc=ffc: e.dma_start(
                    out=A(MO.w1 + (ffc % 4) * c.D, c.D), in_=W1H[ffc]),
                waits=list(waits), sem=sW1[ffc % 4], dma=True)

        def w2_load(ffc, waits=()):
            w2_evs[ffc] = p.ev(
                "scalar", lambda e, ffc=ffc: e.dma_start(
                    out=A(MO.w2 + (ffc % 4) * c.D, c.D),
                    in_=W2_j[ffc * 128:(ffc + 1) * 128, :]),
                waits=list(waits), sem=sW2[ffc % 4], dma=True)

        moe_pre = (list(o_stores[c.BL - 1]) +
                   [sAS[0].ev(), sAS[1].ev()])
        for ffc in range(min(4, c.FFC)):
            w1_load(ffc, waits=moe_pre if ffc == 0 else ())
            w2_load(ffc, waits=moe_pre if ffc == 0 else ())

        w1mm_last = {}
        yb_evac_last = {}
        yst_evs = []
        BSC = 512
        nybc = c.NSLOT // BSC
        ags_done = 0
        for sblk in range(c.NSB):
            xpar = sblk % 2
            xbT = A(MO.xbT + xpar * c.DC * c.BS, c.DC * c.BS)
            xb3 = xbT.rearrange("p (dcv w) -> p dcv w", dcv=c.DC)
            tr_evs = []
            g2 = 0
            for i in range(c.BST):
                stile = sblk * c.BST + i
                for g in range(_cdiv(c.DC, 4)):
                    n = min(4, c.DC - g * 4)
                    tbk = 2 + g2 % 2
                    w0 = [gth_evs[stile]] if g == 0 else []
                    if g2 >= 2:
                        w0.append(tr_evs[g2 - 2])
                    else:
                        w0.append(bk_wait(tbk))
                    if sblk >= 2 and g == 0 and i == 0:
                        w0.append((sT, w1mm_last[sblk - 2]))
                    lastt = None
                    for k in range(n):
                        dc = g * 4 + k
                        if k < n - 1:
                            p.do("tensor", tpose(
                                bank(tbk)[:, k * 128:(k + 1) * 128],
                                A(xbst + (stile % 6) * c.D,
                                  c.D)[:, dc * 128:(dc + 1) * 128],
                                cs_ident[:], k == 0, False),
                                waits=w0 if k == 0 else ())
                        else:
                            lastt = p.ev("tensor", tpose(
                                bank(tbk)[:, k * 128:(k + 1) * 128],
                                A(xbst + (stile % 6) * c.D,
                                  c.D)[:, dc * 128:(dc + 1) * 128],
                                cs_ident[:], k == 0, True),
                                waits=w0 if n == 1 else (), sem=sT)
                    tr_evs.append(p.ev(
                        "vector", lambda e, g=g, n=n, i=i, xb3=xb3, tbk=tbk:
                        e.tensor_copy(
                            xb3[:, g * 4:g * 4 + n,
                                i * 128:(i + 1) * 128],
                            bank(tbk)[:, :n * 128].rearrange(
                                "p (a w) -> p a w", a=n)),
                        waits=[lastt], sem=sV))
                    g2 += 1
                str_evs[stile] = lastt
            xbT_done = tr_evs[-1]
            bk_set(2, tr_evs[-1] if g2 % 2 == 1 else tr_evs[-2])
            bk_set(3, tr_evs[-2] if g2 % 2 == 1 else tr_evs[-1])
            for i in range(c.BST):
                stile = (sblk + 1) * c.BST + i
                if stile < nst_tot and stile not in gth_evs:
                    gather(stile)
            relu_evs = []
            for ffc in range(c.FFC):
                bk = ffc % 2
                w0 = [w1_evs[ffc], xbT_done]
                if ffc >= 2:
                    w0.append(relu_evs[ffc - 2])
                else:
                    w0.append(bk_wait(bk))
                wb = A(MO.w1 + (ffc % 4) * c.D, c.D)
                lastmm = None
                for dc in range(c.DC):
                    if dc < c.DC - 1:
                        p.do("tensor", mm(
                            bank(bk, c.BS), wb[:, dc * 128:(dc + 1) * 128],
                            xbT[:, dc * c.BS:(dc + 1) * c.BS],
                            dc == 0, False), waits=w0 if dc == 0 else ())
                    else:
                        lastmm = p.ev("tensor", mm(
                            bank(bk, c.BS), wb[:, dc * 128:(dc + 1) * 128],
                            xbT[:, dc * c.BS:(dc + 1) * c.BS],
                            dc == 0, True),
                            waits=w0 if c.DC == 1 else (), sem=sT)
                relu_evs.append(p.ev(
                    "scalar", lambda e, ffc=ffc, bk=bk: e.activation(
                        out=hmidT[:, ffc * c.BS:(ffc + 1) * c.BS],
                        in_=bank(bk, c.BS), func=AF.Relu,
                        bias=cs_b1[:, ffc:ffc + 1]),
                    waits=[lastmm], sem=sS))
                bk_set(bk, relu_evs[-1])
                nxt = ffc + 4
                if nxt < c.FFC:
                    w1_load(nxt, waits=[lastmm])
                elif sblk + 1 < c.NSB:
                    w1_load(nxt - c.FFC, waits=[lastmm])
            w1mm_last[sblk] = sT.n
            ntl = c.BST * c.NDW
            w2_lastmm = None
            for ffc in range(c.FFC):
                wb = A(MO.w2 + (ffc % 4) * c.D, c.D)
                w0 = [w2_evs[ffc]]
                if ffc == 0:
                    w0.append(relu_evs[-1])
                    w0.append(b2ld)
                    w0 += [bk_wait(i) for i in range(2, 8)]
                lastmm = None
                for tq in range(c.BST):
                    for dc2 in range(c.NDW):
                        tile = tq * c.NDW + dc2
                        last = tile == ntl - 1
                        if not last:
                            p.do("tensor", mm(
                                bank(tile, c.DW),
                                hmidT[:, ffc * c.BS + tq * 128:
                                      ffc * c.BS + (tq + 1) * 128],
                                wb[:, dc2 * c.DW:(dc2 + 1) * c.DW],
                                ffc == 0, False),
                                waits=w0 if tile == 0 else ())
                        else:
                            lastmm = p.ev("tensor", mm(
                                bank(tile, c.DW),
                                hmidT[:, ffc * c.BS + tq * 128:
                                      ffc * c.BS + (tq + 1) * 128],
                                wb[:, dc2 * c.DW:(dc2 + 1) * c.DW],
                                ffc == 0, ffc == c.FFC - 1), sem=sT)
                nxt = ffc + 4
                if nxt < c.FFC:
                    w2_load(nxt, waits=[lastmm])
                elif sblk + 1 < c.NSB:
                    w2_load(nxt - c.FFC, waits=[lastmm])
                w2_lastmm = lastmm
            last_ev = None
            for tq in range(c.BST):
                for dc2 in range(c.NDW):
                    tile = tq * c.NDW + dc2
                    ow = [w2_lastmm]
                    if dc2 == 0:
                        if tq >= 2:
                            ow.append((sYS[sblk], 16 * tq))
                        elif sblk > 0:
                            ow.append((sYS[sblk - 1], 16 * c.BST))
                    if sblk == 0 and tile == 0:
                        ow += list(o_stores[c.BL - 1])
                    ye = p.ev("vector", lambda e, tile=tile, tq=tq,
                              dc2=dc2: e.tensor_tensor(
                                  A(MO.ybst + (tq % 2) * c.D,
                                    c.D)[:, dc2 * c.DW:(dc2 + 1) * c.DW],
                                  bank(tile, c.DW),
                                  A(MO.b2B,
                                    c.D)[:, dc2 * c.DW:(dc2 + 1) * c.DW],
                                  ALU.add), waits=ow, sem=sV)
                    bk_set(tile, ye)
                yve = p.ev("vector", lambda e: e.drain(), sem=sV)
                last_ev = yve
                yst_evs.append(p.ev(
                    "sync", lambda e, sblk=sblk, tq=tq: e.dma_start(
                        out=yb_loc[sblk * c.BS + tq * 128:
                                   sblk * c.BS + (tq + 1) * 128, :],
                        in_=A(MO.ybst + (tq % 2) * c.D, c.D)),
                    waits=[yve], sem=sYS[sblk], dma=True))
            yb_evac_last[sblk] = last_ev
            p.ev("gpsimd", lambda e, sblk=sblk: e.collective_compute(
                "AllGather", ALU.bypass,
                replica_groups=[list(range(c.n_cores))],
                ins=[yb_loc[sblk * c.BS:(sblk + 1) * c.BS, :]],
                outs=[yb_full[sblk * c.n_cores * c.BS:
                              (sblk + 1) * c.n_cores * c.BS, :]]),
                waits=[(sYS[sblk], 16 * c.BST)], sem=sYAG)
        if c.debug:
            p.ev("sync", lambda e: e.dma_start(
                out=dbg["ybloc"][:, :], in_=yb_loc[:, :]),
                waits=[(sYS[c.NSB - 1], 16 * c.BST)], sem=sDBG, dma=True)

        yag_all = sYAG.ev()
        moe_v_last = sV.ev()
        ar_evs = {}
        g_evs = {}
        comb_v = {}

        def comb_loads(tt):
            par = tt % 4
            if tt < 4:
                w0 = list(o_stores[c.BL - 1]) + [yag_all, moe_v_last]
                gw = [yag_all, moe_v_last]
            else:
                w0 = [comb_v[tt - 4]]
                gw = [comb_v[tt - 4]]
            ar_evs[tt] = p.ev("sync", lambda e, tt=tt, par=par: e.dma_start(
                out=A(MO.arb + par * c.D, c.D),
                in_=attres_dram[tt * 128:(tt + 1) * 128, :]),
                waits=w0, sem=sAR2[tt % 4], dma=True)
            p.ev("gpsimd", lambda e, tt=tt, par=par: e.indirect_dma_start(
                out=A(MO.g0 + par * c.D, c.D), out_offset=None,
                in_=yb_full[:, :],
                in_offset=IndirectOffsetOnAxis(
                    st_row[:, 2 * tt:2 * tt + 1], 0)),
                waits=gw, sem=sGC[tt % 4], dma=True)
            g_evs[tt] = p.ev("gpsimd", lambda e, tt=tt, par=par:
                             e.indirect_dma_start(
                                 out=A(MO.g1 + par * c.D, c.D),
                                 out_offset=None, in_=yb_full[:, :],
                                 in_offset=IndirectOffsetOnAxis(
                                     st_row[:, 2 * tt + 1:2 * tt + 2], 0)),
                             waits=gw, sem=sGC[tt % 4], dma=True)

        y_prev = {}
        for tt in range(min(4, c.NT)):
            comb_loads(tt)
        for tt in range(c.NT):
            par = tt % 4
            p.do("vector", lambda e, tt=tt, par=par: e.tensor_scalar(
                A(MO.g0 + par * c.D, c.D), A(MO.g0 + par * c.D, c.D),
                st_w[:, 2 * tt:2 * tt + 1], None, ALU.mult),
                waits=[g_evs[tt], ar_evs[tt]])
            p.do("vector", lambda e, tt=tt, par=par: e.scalar_tensor_tensor(
                A(MO.g1 + par * c.D, c.D), A(MO.g1 + par * c.D, c.D),
                st_w[:, 2 * tt + 1:2 * tt + 2], A(MO.g0 + par * c.D, c.D),
                ALU.mult, ALU.add), drain=True)
            ove = p.ev("vector", lambda e, tt=tt, par=par: e.tensor_tensor(
                A(MO.outst + par * c.D, c.D), A(MO.arb + par * c.D, c.D),
                A(MO.g1 + par * c.D, c.D), ALU.add),
                waits=[y_prev.get(tt % 4)], sem=sV, drain=True)
            comb_v[tt] = ove
            y_prev[tt % 4] = p.ev(
                "sync", lambda e, tt=tt, par=par: e.dma_start(
                    out=y_j[tt * 128:(tt + 1) * 128, :],
                    in_=A(MO.outst + par * c.D, c.D)), waits=[ove],
                sem=sY[tt % 4], dma=True)
            if tt + 4 < c.NT:
                comb_loads(tt + 4)
        _finish(p, all_sems)
        _emit(nc, p)
    finally:
        for cm in reversed(cms):
            cm.__exit__(None, None, None)
    return nc


def _finish(p, all_sems):
    evs = [s.ev() for s in all_sems]

    def thunk(e, evs=tuple(evs)):
        for s, v in evs:
            if v > 0:
                e.wait_ge(s.h, v)
    p.q["gpsimd"].append(thunk)


def _emit(nc, p):
    with nc.Block() as block:
        @block.sync
        def _(e):
            for f in p.q["sync"]:
                f(e)

        @block.tensor
        def _(e):
            for f in p.q["tensor"]:
                f(e)

        @block.vector
        def _(e):
            for f in p.q["vector"]:
                f(e)

        @block.scalar
        def _(e):
            for f in p.q["scalar"]:
                f(e)

        @block.gpsimd
        def _(e):
            for f in p.q["gpsimd"]:
                f(e)



def kernel(**inputs):
    cfg = CFG()
    ren = {"L0_attn_x": "x", "L0_attn_Wq": "Wq", "L0_attn_bq": "bq",
           "L0_attn_Wk": "Wk", "L0_attn_bk": "bk", "L0_attn_Wv": "Wv",
           "L0_attn_bv": "bv", "L0_attn_Wo": "Wo", "L0_attn_bo": "bo",
           "L0_ffn_gate_w": "gate_w", "L0_ffn_gate_b": "gate_b",
           "L0_ffn_W1": "W1", "L0_ffn_b1": "b1", "L0_ffn_W2": "W2",
           "L0_ffn_b2": "b2"}
    inp = {ren[k]: np.asarray(v) for k, v in inputs.items()}
    nc = build_program(cfg)
    in_maps = [host_prep(cfg, inp, j) for j in range(cfg.n_cores)]
    res = run_bass_kernel_spmd(nc, in_maps, core_ids=list(range(cfg.n_cores)))
    out = np.concatenate([r["y_j"].reshape(cfg.BL, cfg.T, cfg.D)
                          for r in res.results], axis=0)
    return np.ascontiguousarray(out).astype(np.float32)
